# version 08
# speedup vs baseline: 1.9115x; 1.1088x over previous
import math
import numpy as np

import concourse.bass as bass
import concourse.mybir as mybir
from concourse.bass_utils import run_bass_kernel_spmd

F32 = mybir.dt.float32
AF = mybir.ActivationFunctionType
ALU = mybir.AluOpType
AX = mybir.AxisListType.X

B, S, D = 8, 2048, 1024
NH, DH = 16, 64
DL = 256
FF, NE = 4096, 8
NT = S // 128
NC = D // 128
NPAIR = NH // 2
SCALE_MLA = 1.0 / (math.sqrt(1024.0) + 1e-8)
SCALE_XL = 0.125
CLAMP = 10000.0
C_MLA = 1.0
C_XL = 8.0
LN_EPS = 1e-5
SB_CAP = 47000
HAZ_GAP = 3


class _Buf:
    __slots__ = ("w", "r", "name", "dsem", "gsrc", "touched")

    def __init__(self, name=""):
        self.w = []
        self.r = []
        self.name = name
        self.dsem = None
        self.gsrc = []
        self.touched = False


class _Op:
    __slots__ = ("i", "eng", "kind", "fn", "deps", "inc", "val", "waits",
                 "dcls", "superseded", "dbg")


_DMA_CLASSES = {"q": 24, "g": 44, "s": 12, "v": 4, "t": 4}


class Prog:
    def __init__(self):
        self.ops = []
        self.dcls_state = {}
        self.dcls_free = {e: list(range(n)) for e, n in _DMA_CLASSES.items()}

    def release_class(self, buf):
        if buf.dsem is not None:
            self.dcls_free[buf.dsem[0]].append(buf.dsem[1])

    def _mk(self, eng, kind, fn, reads, writes, accs):
        o = _Op()
        o.i = len(self.ops)
        o.eng = eng
        o.kind = kind
        o.fn = fn
        o.inc = False
        o.val = 0
        o.waits = []
        seen = {o.i}
        deps = []

        def add(d):
            if d.i not in seen:
                seen.add(d.i)
                deps.append(d)

        import os as _os
        _dbg_target = _os.environ.get("KDBG_DEP")

        def add2(d, b, why):
            if _dbg_target is not None and d.i == int(_dbg_target):
                print(f"[dep-forensic] op{len(self.ops)} gets dep op{d.i} "
                      f"via buf {b.name!r} ({why})")
            add(d)

        def ghost(b):
            b.touched = True
            if b.gsrc:
                if _os.environ.get("KDBG_GHOST") and b.name == "sel":
                    print(f"[ghost] op{len(self.ops)} buf=sel gsrc="
                          f"{[g.name for g in b.gsrc]}")
                for gb in b.gsrc:
                    gb.r.append(o)
                b.gsrc = []
            elif _os.environ.get("KDBG_GHOST") and b.name == "sel":
                print(f"[ghost] op{len(self.ops)} buf=sel gsrc=EMPTY")

        for b in reads:
            ghost(b)
            for w in b.w:
                add2(w, b, "read<-w")
            b.r.append(o)
        for b in writes:
            ghost(b)
            for w in b.w:
                add2(w, b, "write<-w")
            for r in b.r:
                add2(r, b, "write<-r")
            b.w = [o]
            b.r = []
        for b in accs:
            ghost(b)
            for w in b.w:
                add2(w, b, "acc<-w")
            for r in b.r:
                add2(r, b, "acc<-r")
            b.w = [o]
            b.r = []
        o.deps = deps
        o.dcls = None
        o.superseded = None
        o.dbg = ""
        if kind == "d":
            wbufs = list(writes) + list(accs)
            assert len(wbufs) == 1, f"dma must write exactly one buf ({eng})"
            b = wbufs[0]
            if b.dsem is not None and b.dsem[0] != eng:
                b.dsem = None
            if b.dsem is None:
                free = self.dcls_free[eng]
                if not free:
                    raise RuntimeError(f"out of dma sem classes for queue {eng}")
                ci = free.pop(0)
                b.dsem = (eng, ci)
                st = self.dcls_state.get(b.dsem)
                if st is not None and st["op"] is not None:
                    import os as _os2
                    if _os2.environ.get("KDBG_SUP"):
                        print(f"[takeover] op{len(self.ops)} class {b.dsem} "
                              f"from {[ (ob.name, [x.i for x in ob.w][:5], [x.i for x in ob.r][:50]) for ob in st['bufs'] ]}")
                    add(st["op"])
                    for ob in st["bufs"]:
                        for x in ob.w + ob.r:
                            add(x)
                    st["op"].superseded = o.i
                self.dcls_state[b.dsem] = {"op": o, "bufs": [b]}
            else:
                st = self.dcls_state[b.dsem]
                if st["bufs"] and st["bufs"][0] is not b:
                    if st["op"] is not None:
                        add(st["op"])
                        for ob in st["bufs"]:
                            for x in ob.w + ob.r:
                                add(x)
                        st["op"].superseded = o.i
                st["op"] = o
                st["bufs"] = [b]
            o.dcls = b.dsem
            o.dbg = b.name
        self.ops.append(o)
        return o

    def c(self, eng, fn, reads=(), writes=(), accs=()):
        return self._mk(eng, "c", fn, reads, writes, accs)

    def d(self, eng, fn, reads=(), writes=(), accs=()):
        return self._mk(eng, "d", fn, reads, writes, accs)

    def finalize_and_emit(self, nc, block, sem_of):
        pos = {}
        tmp = {}
        for o in self.ops:
            lst = tmp.setdefault(o.eng, [])
            pos[o.i] = len(lst)
            lst.append(o)

        def covered(o, dp):
            if dp.eng != o.eng or dp.kind != "c" or o.kind != "c":
                return False
            if o.eng == "t":
                return True
            return pos[o.i] - pos[dp.i] > HAZ_GAP

        def semkey(o):
            return o.dcls if o.kind == "d" else (o.eng, "c")

        for o in self.ops:
            if o.kind == "d":
                o.inc = True
            kmax = {}
            for dp in o.deps:
                if covered(o, dp):
                    continue
                key = (dp.eng, dp.kind)
                cur = kmax.get(key)
                if cur is None or dp.i > cur.i:
                    kmax[key] = dp
            for dp in kmax.values():
                dp.inc = True
        cnt = {}
        for o in self.ops:
            if o.inc:
                key = semkey(o)
                cnt[key] = cnt.get(key, 0) + (16 if o.kind == "d" else 1)
                o.val = cnt[key]
        for o in self.ops:
            wmax = {}
            for dp in o.deps:
                while (dp.kind == "d" and dp.superseded is not None
                       and dp.superseded < o.i):
                    dp = self.ops[dp.superseded]
                ok = True
                if dp.kind == "d" and dp.superseded is not None:
                    if dp.superseded == o.i:
                        pass
                    else:
                        sup = self.ops[dp.superseded]
                        ok = False
                        seen2 = set()
                        stk2 = [d for d in sup.deps if d.i >= o.i]
                        while stk2:
                            d = stk2.pop()
                            if d.i in seen2:
                                continue
                            seen2.add(d.i)
                            if d is o or d.eng == o.eng:
                                ok = True
                                break
                            stk2.extend(y for y in d.deps
                                        if y.i >= o.i and y.i not in seen2)
                        if not ok:
                            target = dp
                            visited = set()
                            stack = [x for x in o.deps
                                     if x is not dp and x.i > target.i]
                            while stack:
                                x = stack.pop()
                                if x.i in visited:
                                    continue
                                visited.add(x.i)
                                if x is target:
                                    ok = True
                                    break
                                for y in x.deps:
                                    if y is target:
                                        ok = True
                                        break
                                    if y.i > target.i and y.i not in visited:
                                        stack.append(y)
                                if ok:
                                    break
                            if ok:
                                continue
                if not ok:
                    raise RuntimeError(
                        f"unsound dma wait: op{o.i}({o.eng}) depends on "
                        f"superseded dma op{dp.i}->buf {dp.dbg!r} (superseded "
                        f"at op{dp.superseded}->buf "
                        f"{self.ops[dp.superseded].dbg!r}, class {dp.dcls})")
                if covered(o, dp):
                    continue
                key = semkey(dp)
                if wmax.get(key, 0) < dp.val:
                    wmax[key] = dp.val
            o.waits = sorted(wmax.items(), key=lambda kv: str(kv[0]))

        by_eng = {}
        for o in self.ops:
            by_eng.setdefault(o.eng, []).append(o)

        def body_for(name):
            ops = by_eng.get(name, [])

            def body(eng):
                hw = {}
                for o in ops:
                    for key, val in o.waits:
                        if hw.get(key, 0) < val:
                            eng.wait_ge(sem_of[key], val)
                            hw[key] = val
                    inst = o.fn(eng) if o.fn is not None else None
                    if o.inc:
                        inst.then_inc(sem_of[semkey(o)],
                                      16 if o.kind == "d" else 1)
            return body

        block.tensor(body_for("t"))
        block.vector(body_for("v"))
        block.scalar(body_for("s"))
        block.gpsimd(body_for("g"))
        block.sync(body_for("q"))
        return cnt


class SBAlloc:
    def __init__(self, tensor_ap, cap, prog=None):
        self.t = tensor_ap
        self.cap = cap
        self.prog = prog
        self.free_list = [(0, cap)]
        self.ghosts = []
        self.peak = 0
        self.used = 0

    def alloc(self, name, size):
        if not hasattr(self, "live"):
            self.live = {}
        for i, (off, sz) in enumerate(self.free_list):
            if sz >= size:
                if sz == size:
                    self.free_list.pop(i)
                else:
                    self.free_list[i] = (off + size, sz - size)
                buf = _Buf(name)
                seen = set()
                import os as _os3
                _tgt = _os3.environ.get("KDBG_ALLOC")
                for goff, gsz, gbufs in self.ghosts:
                    if goff < off + size and off < goff + gsz:
                        for gbuf in gbufs:
                            if _tgt and name == "sel":
                                ids = [x.i for x in gbuf.w + gbuf.r]
                                if int(_tgt) in ids:
                                    print(f"[alloc-forensic] sel inherits "
                                          f"op{_tgt} from gbuf {gbuf.name!r} "
                                          f"(w={[x.i for x in gbuf.w][:6]})")
                            if gbuf not in buf.gsrc:
                                buf.gsrc.append(gbuf)
                            for x in gbuf.w + gbuf.r:
                                if x.i not in seen:
                                    seen.add(x.i)
                                    buf.w.append(x)
                self.used += size
                self.peak = max(self.peak, self.used)
                self.live[(name, off)] = size
                return Region(self, name, off, size, buf)
        raise RuntimeError(f"SBUF alloc failed: {name} size {size}, "
                           f"free={self.free_list}, live={getattr(self, 'live', {})}")

    def release(self, region):
        self.live.pop((region.name, region.off), None)
        if self.prog is not None:
            for b in [region.buf] + list(region.extra):
                self.prog.release_class(b)
        tb = [b for b in [region.buf] + list(region.extra) if b.touched]
        if tb:
            off, size = region.off, region.size
            self.ghosts = [g for g in self.ghosts
                           if not (off <= g[0] and g[0] + g[1] <= off + size)]
            self.ghosts.append((off, size, tb))
        self.used -= region.size
        self.free_list.append((region.off, region.size))
        self.free_list.sort()
        merged = []
        for off, sz in self.free_list:
            if merged and merged[-1][0] + merged[-1][1] == off:
                merged[-1] = (merged[-1][0], merged[-1][1] + sz)
            else:
                merged.append((off, sz))
        self.free_list = merged


class Region:
    __slots__ = ("sba", "name", "off", "size", "buf", "extra")

    def __init__(self, sba, name, off, size, buf):
        self.sba = sba
        self.name = name
        self.off = off
        self.size = size
        self.buf = buf
        self.extra = []

    def mkbuf(self, name):
        b = _Buf(name)
        b.w = list(self.buf.w)
        b.gsrc = list(self.buf.gsrc)
        self.extra.append(b)
        return b

    def mkbufs(self, name, n):
        return [self.mkbuf(f"{name}{i}") for i in range(n)]

    def ap(self, p0=0, pn=128, c0=0, cn=None):
        if cn is None:
            cn = self.size - c0
        assert c0 + cn <= self.size, (self.name, c0, cn, self.size)
        return self.sba.t[p0:p0 + pn, self.off + c0:self.off + c0 + cn]

    def free(self):
        self.sba.release(self)


def _panels(w):
    k, n = np.asarray(w).shape
    return np.ascontiguousarray(np.asarray(w, np.float32).reshape(
        k, n // 128, 128).transpose(1, 0, 2))


def _host_consts():
    idx = np.arange(S, dtype=np.float64)
    decay = np.exp(-0.01 * 0.01 * np.abs(idx[:, None] - idx[None, :])).astype(np.float32)
    pos = np.arange(S - 1, -1, -1, dtype=np.float64)
    inv_freq = 1.0 / 10000.0 ** (np.arange(0, D, 2, dtype=np.float64) / D)
    sinusoid = pos[:, None] * inv_freq[None, :]
    pe = np.concatenate([np.sin(sinusoid), np.cos(sinusoid)], axis=-1)
    peT = np.ascontiguousarray(pe.T.astype(np.float32))
    ident = np.concatenate([np.eye(128, dtype=np.float32),
                            np.tile(np.eye(64, dtype=np.float32), (2, 1))], axis=1)
    ident = np.ascontiguousarray(ident)
    tri = np.tril(np.ones((128, 128), np.float32))
    esel = np.zeros((8, 8 * 128), np.float32)
    for e in range(8):
        esel[e, e * 128:(e + 1) * 128] = 1.0
    return decay, peT, ident, tri, esel


def build_program(tap=None, upto="full"):
    nc = bass.Bass("TRN2", target_bir_lowering=False, debug=False)

    dx = nc.dram_tensor("x", [S, D], F32, kind="ExternalInput").ap()
    dout = nc.dram_tensor("out", [S, D], F32, kind="ExternalOutput").ap()

    def win(name, shape):
        return nc.dram_tensor(name, list(shape), F32, kind="ExternalInput").ap()

    w_q = win("w_q", (NC, D, 128))
    w_k = win("w_k", (NC, D, 128))
    w_v = win("w_v", (NC, D, 128))
    w_o = win("w_o", (NC, D, 128))
    w_dn = win("w_dn", (2, D, 128))
    w_uk = win("w_uk", (NC, DL, 128))
    w_uv = win("w_uv", (NC, DL, 128))
    w_qkv = win("w_qkv", (24, D, 128))
    w_fo = win("w_fo", (NC, D, 128))
    w_r = win("w_r", (NC, D, 128))
    d_ub = win("u_bias", (128, 8))
    d_vb = win("v_bias", (128, 8))
    d_ln = win("ln_gb", (6, D))
    w_ff1 = win("w_ff1", (FF // 128, D, 128))
    d_bff1 = win("b_ff1", (128, 32))
    w_ff2 = win("w_ff2", (NC, FF, 128))
    d_bff2 = win("b_ff2", (128, 8))
    w_g = win("w_g", (D, NE))
    w_e1 = win("w_e1", (NE, FF // 128, D, 128))
    d_be1 = win("b_e1", (NE, 128, 32))
    w_e2 = win("w_e2", (NE, NC, FF, 128))
    d_be2 = win("b_e2", (NE, 128, 8))
    w_out = win("w_out", (NC, D, 128))
    d_esel = win("c_esel", (8, 8 * 128))
    d_decay = win("c_decay", (S, S))
    d_peT = win("c_peT", (D, S))
    d_ident = win("c_ident", (128, 192))
    d_tri = win("c_tri", (128, 128))

    def scratch(name, shape):
        return nc.dram_tensor(name, list(shape), F32, kind="Internal").ap()

    s_xn = scratch("s_xn", (D, S))
    s_x1 = scratch("s_x1", (S, D))
    s_y = scratch("s_y", (D, S))
    s_R = scratch("s_R", (D, S))
    s_h1 = scratch("s_h1", (D, S))
    s_h2 = scratch("s_h2", (D, S))
    s_h2p = scratch("s_h2p", (D, S))
    s_moe = scratch("s_moe", (D, S))

    def scratch_bf(name, shape):
        return nc.dram_tensor(name, list(shape), mybir.dt.bfloat16,
                              kind="Internal").ap()

    s_f1h = scratch_bf("s_f1h", (FF // 128, D, 128))
    s_f1l = scratch_bf("s_f1l", (FF // 128, D, 128))
    s_f2h = scratch_bf("s_f2h", (NC, FF, 128))
    s_f2l = scratch_bf("s_f2l", (NC, FF, 128))
    s_e1h = scratch_bf("s_e1h", (NE, FF // 128, D, 128))
    s_e1l = scratch_bf("s_e1l", (NE, FF // 128, D, 128))
    s_e2h = scratch_bf("s_e2h", (NE, NC, FF, 128))
    s_e2l = scratch_bf("s_e2l", (NE, NC, FF, 128))
    s_h1h = scratch_bf("s_h1h", (D, S))
    s_h1l = scratch_bf("s_h1l", (D, S))
    s_h2h = scratch_bf("s_h2h", (D, S))
    s_h2l = scratch_bf("s_h2l", (D, S))
    s_bd = scratch("s_bd", (2, 128 * S))

    d_tap = None
    if tap is not None:
        d_tap = nc.dram_tensor("tap", [S, D], F32, kind="ExternalOutput").ap()

    p = Prog()
    sbuf_t = nc.alloc_sbuf_tensor("arena", [128, SB_CAP], F32).ap()
    sba = SBAlloc(sbuf_t, SB_CAP, prog=p)

    r_ident = sba.alloc("ident", 192)
    r_tri = sba.alloc("tri", 128)
    r_ones = sba.alloc("ones", 128)
    r_stats = sba.alloc("stats", 16)
    r_acc = sba.alloc("acc", 16384)

    ps = [nc.alloc_psum_tensor(f"ps{i}", [128, 512], F32).ap() for i in range(8)]
    pb = [_Buf(f"ps{i}") for i in range(8)]

    dram_bufs = {}

    def dbuf(name):
        if name not in dram_bufs:
            dram_bufs[name] = _Buf(name)
        return dram_bufs[name]

    W_IN = dbuf("w_in")

    def load(dst_ap, src_ap, reads=(), writes=(), accs=(), eng="q"):
        return p.d(eng, lambda e, o=dst_ap, i=src_ap: e.dma_start(out=o, in_=i),
                   reads=reads, writes=writes, accs=accs)

    def store(dst_ap, src_ap, reads=(), writes=(), accs=(), eng="g"):
        return p.d(eng, lambda e, o=dst_ap, i=src_ap: e.dma_start(out=o, in_=i),
                   reads=reads, writes=writes, accs=accs)

    def mm(ps_ap, ps_buf, lhsT, rhs, reads, start, stop):
        fn = lambda e, o=ps_ap, a=lhsT, b=rhs, s0=start, s1=stop: e.matmul(
            o, a, b, start=s0, stop=s1)
        if start:
            return p.c("t", fn, reads=reads, writes=(ps_buf,))
        return p.c("t", fn, reads=reads, accs=(ps_buf,))

    def transpose(ps_ap, ps_buf, in_ap, reads, start=True, pdim=128, pbase=0):
        if pdim == 128:
            ident_ap = r_ident.ap(0, 128, 0, 128)
        elif pdim == 64:
            ident_ap = r_ident.ap(pbase, 64, 128, 64)
        else:
            assert pbase == 0 and pdim <= 128
            ident_ap = r_ident.ap(0, pdim, 0, pdim)
        fn = lambda e, o=ps_ap, i=in_ap, idp=ident_ap: e.transpose(o, i, idp)
        if start:
            return p.c("t", fn, reads=tuple(reads) + (r_ident.buf,), writes=(ps_buf,))
        return p.c("t", fn, reads=tuple(reads) + (r_ident.buf,), accs=(ps_buf,))

    def copy(eng, dst_ap, src_ap, reads, writes=(), accs=()):
        return p.c(eng, lambda e, o=dst_ap, i=src_ap: e.tensor_copy(out=o, in_=i),
                   reads=reads, writes=writes, accs=accs)

    def load_panel_bf(region, coff, dram_t, K, idx, buf, eng="q", k0=0, kn=None):
        if kn is None:
            kn = K
        kc = kn // 128
        src_ = dram_t[idx] if isinstance(idx, int) else dram_t[idx[0], idx[1]]
        a = src_[k0:k0 + kn, :].rearrange("(c pp) j -> pp c j", pp=128)
        dst = region.ap(0, 128, coff, kc * 64).bitcast(
            mybir.dt.bfloat16).rearrange("pp (c j) -> pp c j", j=128)
        return load(dst, a, reads=(W_IN,), writes=(buf,), eng=eng)

    def load_panel(region, coff, dram_t, K, idx, buf=None, eng="q", k0=0,
                   kn=None, accbuf=None):
        if kn is None:
            kn = K
        kc = kn // 128
        src = dram_t[idx] if isinstance(idx, int) else dram_t[idx[0], idx[1]]
        a = src[k0:k0 + kn, :].rearrange("(c pp) j -> pp c j", pp=128)
        dst = region.ap(0, 128, coff, kc * 128).rearrange("pp (c j) -> pp c j", j=128)
        if accbuf is not None:
            return load(dst, a, reads=(W_IN,), accs=(accbuf,), eng=eng)
        if buf is None:
            return load(dst, a, reads=(W_IN,), accs=(region.buf,), eng=eng)
        return load(dst, a, reads=(W_IN,), writes=(buf,), eng=eng)

    load(r_ident.ap(0, 128, 0, 192), d_ident, reads=(W_IN,), writes=(r_ident.buf,))
    load(r_tri.ap(0, 128, 0, 128), d_tri, reads=(W_IN,), writes=(r_tri.buf,))
    p.c("v", lambda e: e.memset(r_ones.ap(), 1.0), writes=(r_ones.buf,))

    r_cimm = sba.alloc("cimm", 8)
    C_NMLA, C_NXL, C_SXL, C_SMLA = (r_cimm.ap(0, 128, i, 1) for i in range(4))
    for i, val in enumerate((-C_MLA, -C_XL, SCALE_XL, SCALE_MLA)):
        p.c("v", lambda e, a=r_cimm.ap(0, 128, i, 1), v=float(val): e.memset(a, v),
            writes=(r_cimm.buf,) if i == 0 else (), accs=(r_cimm.buf,) if i > 0 else ())

    def ln_rowwise(x_ap, x_buf, out_ap, out_buf, scr_ap, scr_buf, width, gb=None):
        st = r_stats
        sbuf = st.buf
        ssum, ssq = st.ap(0, 128, 0, 1), st.ap(0, 128, 1, 1)
        mean, var = st.ap(0, 128, 2, 1), st.ap(0, 128, 3, 1)
        rstd, nmean = st.ap(0, 128, 4, 1), st.ap(0, 128, 5, 1)
        m2 = st.ap(0, 128, 6, 1)
        p.c("v", lambda e, i=x_ap, o=ssum: e.reduce_sum(o, i, axis=AX),
            reads=(x_buf,), accs=(sbuf,))
        p.c("s", lambda e, i=x_ap, o=scr_ap, a=ssq: e.activation(
            o, i, AF.Square, accum_out=a),
            reads=(x_buf,), writes=(scr_buf,), accs=(sbuf,))
        p.c("v", lambda e, o=mean, i=ssum, w=width: e.tensor_scalar(
            out=o, in0=i, scalar1=1.0 / w, scalar2=None, op0=ALU.mult), accs=(sbuf,))
        p.c("v", lambda e, o=m2, i=mean: e.tensor_tensor(out=o, in0=i, in1=i, op=ALU.mult),
            accs=(sbuf,))
        p.c("v", lambda e, o=var, i=ssq, w=width: e.tensor_scalar(
            out=o, in0=i, scalar1=1.0 / w, scalar2=None, op0=ALU.mult), accs=(sbuf,))
        p.c("v", lambda e, o=var, a=m2: e.tensor_tensor(out=o, in0=o, in1=a, op=ALU.subtract),
            accs=(sbuf,))
        p.c("v", lambda e, o=var: e.tensor_scalar(
            out=o, in0=o, scalar1=float(LN_EPS), scalar2=None, op0=ALU.add),
            accs=(sbuf,))
        p.c("s", lambda e, o=var: e.activation(o, o, AF.Sqrt), accs=(sbuf,))
        p.c("v", lambda e, o=rstd, i=var: e.reciprocal(o, i), accs=(sbuf,))
        p.c("v", lambda e, o=nmean, i=mean: e.tensor_scalar(
            out=o, in0=i, scalar1=-1.0, scalar2=None, op0=ALU.mult), accs=(sbuf,))
        p.c("v", lambda e, o=out_ap, i=x_ap, a=nmean, b=rstd: e.tensor_scalar(
            out=o, in0=i, scalar1=a, scalar2=b, op0=ALU.add, op1=ALU.mult),
            reads=(x_buf, sbuf), accs=(out_buf,))
        if gb is not None:
            g_ap, b_ap, g_buf = gb
            p.c("v", lambda e, o=out_ap, g=g_ap: e.tensor_tensor(
                out=o, in0=o, in1=g, op=ALU.mult), reads=(g_buf,), accs=(out_buf,))
            p.c("v", lambda e, o=out_ap, g=b_ap: e.tensor_tensor(
                out=o, in0=o, in1=g, op=ALU.add), reads=(g_buf,), accs=(out_buf,))

    def replicate_ln_params(idx_g, idx_b):
        r_gb = sba.alloc("lngb", 2048)
        tmp = sba.alloc("lntmp", 2048)
        tb = tmp.mkbuf("lntmp")
        load(tmp.ap(0, 1, 0, 1024), d_ln[idx_g:idx_g + 1, :], reads=(W_IN,), writes=(tb,))
        load(tmp.ap(0, 1, 1024, 1024), d_ln[idx_b:idx_b + 1, :], reads=(W_IN,), accs=(tb,))
        for k in range(4):
            mm(ps[7][:, 0:512], pb[7], r_ones.ap(0, 1, 0, 128),
               tmp.ap(0, 1, k * 512, 512), reads=(r_ones.buf, tb),
               start=True, stop=True)
            copy("v", r_gb.ap(0, 128, k * 512, 512), ps[7][:, 0:512],
                 reads=(pb[7],), accs=(r_gb.buf,))
        tmp.free()
        return r_gb

    def fm_to_tm_tile(t, get_block, ta, tbuf):
        for g in range(2):
            pi = 4 + g
            for j in range(4):
                kc = g * 4 + j
                blk, rds = get_block(kc)
                transpose(ps[pi][:, j * 128:(j + 1) * 128], pb[pi], blk,
                          reads=rds, start=(j == 0))
            copy("v", ta[:, g * 512:(g + 1) * 512], ps[pi][:, 0:512],
                 reads=(pb[pi],), accs=(tbuf,))

    def tm_to_fm_store(t, na, nbuf, rf, fb, sl, dst_dram, dst_dbuf,
                       bf_pair=None, rbf=None, bfb=None):
        for g in range(2):
            pi = 6 + g
            for j in range(4):
                kc = g * 4 + j
                transpose(ps[pi][:, j * 128:(j + 1) * 128], pb[pi],
                          na[:, kc * 128:(kc + 1) * 128], reads=(nbuf,),
                          start=(j == 0))
            fa = rf.ap(0, 128, sl * 1024 + g * 512, 512)
            copy("v", fa, ps[pi][:, 0:512], reads=(pb[pi],), writes=(fb[sl],))
            dstap = bass.AP(dst_dram.tensor,
                            dst_dram.offset + (g * 4 * 128) * S + t * 128,
                            [[S, 128], [128 * S, 4], [1, 128]])
            store(dstap, fa.rearrange("pp (c j) -> pp c j", j=128),
                  reads=(fb[sl],), accs=(dst_dbuf,))
            if bf_pair is not None:
                hi_d, lo_d = bf_pair
                ha = rbf.ap(0, 128, sl * 512 + g * 256, 256).bitcast(
                    mybir.dt.bfloat16)
                la = rbf.ap(0, 128, 1024 + sl * 512 + g * 256, 256).bitcast(
                    mybir.dt.bfloat16)
                p.c("s", lambda e, o=ha, i=fa: e.activation(o, i, AF.Copy),
                    reads=(fb[sl],), accs=(bfb[sl],))
                p.c("v", lambda e, o=la, a=fa, b=ha:
                    e.tensor_tensor(out=o, in0=a, in1=b, op=ALU.subtract),
                    reads=(fb[sl],), accs=(bfb[sl],))
                for dd, aa in ((hi_d, ha), (lo_d, la)):
                    dstap2 = bass.AP(dd.tensor,
                                     dd.offset + (g * 4 * 128) * S + t * 128,
                                     [[S, 128], [128 * S, 4], [1, 128]])
                    store(dstap2, aa.rearrange("pp (c j) -> pp c j", j=128),
                          reads=(bfb[sl],), accs=(dst_dbuf,))

    def make_split_shards():
        jobs = []
        for n in range(FF // 128):
            jobs.append((w_ff1, n, D, s_f1h, s_f1l, "ws_f1"))
        for m in range(NC):
            jobs.append((w_ff2, m, FF, s_f2h, s_f2l, "ws_f2"))
        for e in range(NE):
            for n in range(FF // 128):
                jobs.append((w_e1, (e, n), D, s_e1h, s_e1l, f"ws_e1_{e}"))
            for m in range(NC):
                jobs.append((w_e2, (e, m), FF, s_e2h, s_e2l, f"ws_e2_{e}"))
        r_ws = sba.alloc("wsplit", 2048)
        wsb = r_ws.mkbufs("wsplit", 1)

        def do_job(job, slot):
            srct, idx, K, hid, lod, dbn = job
            coff = slot * 2048
            for k0 in range(0, K, 1024):
                load_panel(r_ws, coff, srct, K, idx, buf=wsb[slot], eng="s",
                           k0=k0, kn=1024)
                p.c("s", lambda e, o=r_ws.ap(0, 128, coff + 1024, 512).bitcast(
                    mybir.dt.bfloat16), i=r_ws.ap(0, 128, coff, 1024):
                    e.activation(o, i, AF.Copy), reads=(wsb[slot],),
                    accs=(wsb[slot],))
                p.c("v", lambda e, o=r_ws.ap(0, 128, coff + 1536, 512).bitcast(
                    mybir.dt.bfloat16), a=r_ws.ap(0, 128, coff, 1024),
                    b=r_ws.ap(0, 128, coff + 1024, 512).bitcast(mybir.dt.bfloat16):
                    e.tensor_tensor(out=o, in0=a, in1=b, op=ALU.subtract),
                    accs=(wsb[slot],))
                dsth = hid[idx] if isinstance(idx, int) else hid[idx[0], idx[1]]
                dstl = lod[idx] if isinstance(idx, int) else lod[idx[0], idx[1]]
                store(dsth[k0:k0 + 1024, :].rearrange("(c pp) j -> pp c j", pp=128),
                      r_ws.ap(0, 128, coff + 1024, 512).bitcast(
                          mybir.dt.bfloat16).rearrange("pp (c j) -> pp c j", j=128),
                      reads=(wsb[slot],), accs=(dbuf(dbn),), eng="g")
                store(dstl[k0:k0 + 1024, :].rearrange("(c pp) j -> pp c j", pp=128),
                      r_ws.ap(0, 128, coff + 1536, 512).bitcast(
                          mybir.dt.bfloat16).rearrange("pp (c j) -> pp c j", j=128),
                      reads=(wsb[slot],), accs=(dbuf(dbn),), eng="g")

        state = {"i": 0, "slot": 0}

        def emit(njobs):
            while njobs > 0 and state["i"] < len(jobs):
                do_job(jobs[state["i"]], state["slot"])
                state["i"] += 1
                state["slot"] = 0
                njobs -= 1

        def finish():
            emit(len(jobs))
            r_ws.free()
        return emit, finish

    def stage_ln_x():
        rx = sba.alloc("s1x", 2048)
        rn = sba.alloc("s1n", 2048)
        rs = sba.alloc("s1s", 1024)
        rf = sba.alloc("s1f", 2048)
        xb = rx.mkbufs("s1x", 2)
        nb = rn.mkbufs("s1n", 2)
        fb = rf.mkbufs("s1f", 2)
        for t in range(NT):
            sl = t % 2
            xa = rx.ap(0, 128, sl * 1024, 1024)
            na = rn.ap(0, 128, sl * 1024, 1024)
            load(xa, dx[t * 128:(t + 1) * 128, :], reads=(W_IN,), writes=(xb[sl],))
            ln_rowwise(xa, xb[sl], na, nb[sl], rs.ap(), rs.buf, D)
            if tap == "xn":
                store(d_tap[t * 128:(t + 1) * 128, :], na, reads=(nb[sl],),
                      accs=(dbuf("tap"),))
            tm_to_fm_store(t, na, nb[sl], rf, fb, sl, s_xn, dbuf("xn"))
        rx.free(); rn.free(); rs.free(); rf.free()

    def stage_lat():
        r_lat = sba.alloc("lat", 2 * 2048)
        r_wkv = sba.alloc("wkv", 2 * 3 * 1024)
        wkvb = r_wkv.mkbufs("wkv", 2)
        r_wd = sba.alloc("wd", 2 * 1024)
        r_xs = sba.alloc("s2x", 8 * 512)
        r_kv = sba.alloc("s2kv", 2 * 512)
        xsb = r_xs.mkbufs("s2x", 8)
        kvb = r_kv.mkbufs("s2kv", 2)
        for lp in range(2):
            load_panel(r_wd, lp * 1024, w_dn, D, lp)
        for stripe in range(4):
            q0 = stripe * 512
            for n in range(NC):
                wko = (n % 2) * 3072
                wkb = wkvb[n % 2]
                load_panel(r_wkv, wko, w_k, D, n, buf=wkb)
                load_panel(r_wkv, wko + 1024, w_v, D, n, buf=None, accbuf=wkb)
                p.c("v", lambda e, o=r_wkv.ap(0, 128, wko + 2048, 1024),
                    a=r_wkv.ap(0, 128, wko, 1024), b=r_wkv.ap(0, 128, wko + 1024, 1024):
                    e.tensor_tensor(out=o, in0=a, in1=b, op=ALU.add),
                    accs=(wkb,))
                kvp = n % 2
                for kc in range(NC):
                    sl = kc
                    xa = r_xs.ap(0, 128, sl * 512, 512)
                    load(xa, s_xn[kc * 128:(kc + 1) * 128, q0:q0 + 512],
                         reads=(dbuf("xn"),), writes=(xsb[sl],))
                    mm(ps[0 if kvp == 0 else 3][:, 0:512], pb[0 if kvp == 0 else 3],
                       r_wkv.ap(0, 128, wko + 2048 + kc * 128, 128),
                       xa, reads=(wkb, xsb[sl]), start=(kc == 0),
                       stop=(kc == NC - 1))
                copy("v", r_kv.ap(0, 128, kvp * 512, 512),
                     ps[0 if kvp == 0 else 3][:, 0:512],
                     reads=(pb[0 if kvp == 0 else 3],), writes=(kvb[kvp],))
                for lp in range(2):
                    mm(ps[1 + lp][:, 0:512], pb[1 + lp],
                       r_wd.ap(0, 128, lp * 1024 + n * 128, 128),
                       r_kv.ap(0, 128, kvp * 512, 512),
                       reads=(r_wd.buf, kvb[kvp]), start=(n == 0), stop=(n == NC - 1))
            for lp in range(2):
                copy("v", r_lat.ap(0, 128, lp * 2048 + q0, 512), ps[1 + lp][:, 0:512],
                     reads=(pb[1 + lp],), accs=(r_lat.buf,))
        r_wkv.free(); r_wd.free(); r_xs.free(); r_kv.free()
        return r_lat

    def wo_accumulate(hp, r_of, ofb, w_dram, first_pair):
        r_wt = sba.alloc("wot", 2 * 128)
        wtb = r_wt.mkbufs("wo", 2)
        for n in range(NC):
            sl = n % 2
            wa = r_wt.ap(0, 128, sl * 128, 128)
            load(wa, w_dram[n][hp * 128:(hp + 1) * 128, :], reads=(W_IN,),
                 writes=(wtb[sl],))
            for stripe in range(4):
                pi = 4 + (stripe % 2)
                mm(ps[pi][:, 0:512], pb[pi], wa, r_of.ap(0, 128, stripe * 512, 512),
                   reads=(wtb[sl], ofb), start=True, stop=True)
                dst = r_acc.ap(0, 128, n * 2048 + stripe * 512, 512)
                if first_pair:
                    copy("v", dst, ps[pi][:, 0:512], reads=(pb[pi],), accs=(r_acc.buf,))
                else:
                    p.c("v", lambda e, o=dst, i=ps[pi][:, 0:512]:
                        e.tensor_tensor(out=o, in0=o, in1=i, op=ALU.add),
                        reads=(pb[pi],), accs=(r_acc.buf,))
        r_wt.free()

    def stage_mla(r_lat, split_emit=lambda n: None):
        r_q = sba.alloc("mq", 2048)
        r_kr = sba.alloc("mkr", 2048)
        r_vr = sba.alloc("mvr", 2048)
        r_vs = sba.alloc("mvs", 2 * 16 * 65)
        r_pt = sba.alloc("mpt", 4 * 512)
        r_dk = sba.alloc("mdk", 2 * 512)
        r_wp = sba.alloc("mwp", 2 * 1024)
        r_xs = sba.alloc("mxs", 8 * 512)
        r_oh = sba.alloc("moh", 2 * 512)
        r_rz = sba.alloc("mrz", 2 * 512)
        r_of = sba.alloc("mof", 2048)
        qb, krb, vrb = r_q.mkbuf("mq"), r_kr.mkbuf("mkr"), r_vr.mkbuf("mvr")
        vsb = r_vs.mkbufs("mvs", 2)
        ptb = r_pt.mkbufs("mpt", 4)
        dkb = r_dk.mkbufs("mdk", 2)
        xsb = r_xs.mkbufs("mxs", 8)
        ohb = r_oh.mkbufs("moh", 2)
        rzb = r_rz.mkbufs("mrz", 2)
        ofb = r_of.mkbuf("mof")

        for hp in range(NPAIR):
            split_emit(4)
            load_panel(r_wp, 0, w_q, D, hp)
            for stripe in range(4):
                q0 = stripe * 512
                qpi = stripe % 2
                for kc in range(NC):
                    sl = kc
                    xa = r_xs.ap(0, 128, sl * 512, 512)
                    load(xa, s_xn[kc * 128:(kc + 1) * 128, q0:q0 + 512],
                         reads=(dbuf("xn"),), writes=(xsb[sl],))
                    mm(ps[qpi][:, 0:512], pb[qpi], r_wp.ap(0, 128, kc * 128, 128), xa,
                       reads=(r_wp.buf, xsb[sl]), start=(kc == 0), stop=(kc == NC - 1))
                p.c("s", lambda e, o=r_q.ap(0, 128, q0, 512), i=ps[qpi][:, 0:512],
                    sc=C_SMLA: e.activation(o, i, AF.Copy, scale=sc),
                    reads=(pb[qpi], r_cimm.buf), accs=(qb,))
            load_panel(r_wp, 0, w_uk, DL, hp)
            load_panel(r_wp, 1024, w_uv, DL, hp)
            for stripe in range(4):
                q0 = stripe * 512
                for lp in range(2):
                    mm(ps[1][:, 0:512], pb[1], r_wp.ap(0, 128, lp * 128, 128),
                       r_lat.ap(0, 128, lp * 2048 + q0, 512),
                       reads=(r_wp.buf, r_lat.buf), start=(lp == 0), stop=(lp == 1))
                    mm(ps[2][:, 0:512], pb[2], r_wp.ap(0, 128, 1024 + lp * 128, 128),
                       r_lat.ap(0, 128, lp * 2048 + q0, 512),
                       reads=(r_wp.buf, r_lat.buf), start=(lp == 0), stop=(lp == 1))
                copy("v", r_kr.ap(0, 128, q0, 512), ps[1][:, 0:512],
                     reads=(pb[1],), accs=(krb,))
                copy("v", r_vr.ap(0, 128, q0, 512), ps[2][:, 0:512],
                     reads=(pb[2],), accs=(vrb,))
            for h in range(2):
                ones_cols = r_vs.ap(0, 128, h * 1040, 1040).rearrange(
                    "pp (kt c) -> pp kt c", c=65)[:, :, 64:65]
                p.c("v", lambda e, a=ones_cols: e.memset(a, 1.0), accs=(vsb[h],))
                for kt in range(NT):
                    pi = 3 + (kt % 2)
                    transpose(ps[pi][:, 0:64], pb[pi],
                              r_vr.ap(h * 64, 64, kt * 128, 128), reads=(vrb,),
                              pdim=64, pbase=h * 64)
                    copy("v", r_vs.ap(0, 128, h * 1040 + kt * 65, 64), ps[pi][:, 0:64],
                         reads=(pb[pi],), accs=(vsb[h],))
            for qq in range(4):
                split_emit(2)
                q0 = qq * 512

                def emit_st(kt):
                    dsl = kt % 2
                    dk = r_dk.ap(0, 128, dsl * 512, 512)
                    load(dk, d_decay[kt * 128:(kt + 1) * 128, q0:q0 + 512],
                         reads=(W_IN,), writes=(dkb[dsl],))
                    for h in range(2):
                        spi = h * 2 + (kt % 2)
                        mm(ps[spi][:, 0:512], pb[spi],
                           r_kr.ap(h * 64, 64, kt * 128, 128),
                           r_q.ap(h * 64, 64, q0, 512),
                           reads=(krb, qb), start=True, stop=True)

                def emit_consume(kt):
                    dsl = kt % 2
                    dk = r_dk.ap(0, 128, dsl * 512, 512)
                    for h in range(2):
                        spi = h * 2 + (kt % 2)
                        pt = r_pt.ap(0, 128, spi * 512, 512)
                        pbuf = ptb[spi]
                        p.c("v", lambda e, o=pt, i=ps[spi][:, 0:512]:
                            e.tensor_scalar(out=o, in0=i, scalar1=CLAMP,
                                            scalar2=-CLAMP, op0=ALU.min, op1=ALU.max),
                            reads=(pb[spi],), writes=(pbuf,))
                        p.c("v", lambda e, o=pt, i=dk:
                            e.tensor_tensor(out=o, in0=o, in1=i, op=ALU.mult),
                            reads=(dkb[dsl],), accs=(pbuf,))
                        p.c("s", lambda e, o=pt, b=C_NMLA:
                            e.activation(o, o, AF.Exp, bias=b),
                            reads=(r_cimm.buf,), accs=(pbuf,))
                        mm(ps[6 + h][0:65, 0:512], pb[6 + h],
                           r_vs.ap(0, 128, h * 1040 + kt * 65, 65), pt,
                           reads=(vsb[h], pbuf), start=(kt == 0),
                           stop=(kt == NT - 1))

                emit_st(0)
                for kt in range(1, NT):
                    emit_st(kt)
                    emit_consume(kt - 1)
                emit_consume(NT - 1)
                for h in range(2):
                    oh = r_oh.ap(0, 65, h * 512, 512)
                    copy("v", oh, ps[6 + h][0:65, 0:512], reads=(pb[6 + h],),
                         writes=(ohb[h],))
                    rz1 = r_rz.ap(64, 1, h * 512, 512)
                    p.c("v", lambda e, o=rz1, i=r_oh.ap(64, 1, h * 512, 512):
                        e.reciprocal(o, i), reads=(ohb[h],), writes=(rzb[h],))
                    mm(ps[h][0:64, 0:512], pb[h], r_ones.ap(64, 1, 0, 64), rz1,
                       reads=(r_ones.buf, rzb[h]), start=True, stop=True)
                    p.c("v", lambda e, o=r_of.ap(h * 64, 64, q0, 512),
                        a=r_oh.ap(0, 64, h * 512, 512), b=ps[h][0:64, 0:512]:
                        e.tensor_tensor(out=o, in0=a, in1=b, op=ALU.mult),
                        reads=(ohb[h], pb[h]), accs=(ofb,))
            wo_accumulate(hp, r_of, ofb, w_o, first_pair=(hp == 0))
        for r in (r_q, r_kr, r_vr, r_vs, r_pt, r_dk, r_wp, r_xs, r_oh, r_rz, r_of):
            r.free()

    def stage_x1_y():
        rx = sba.alloc("s4x", 2048)
        r1 = sba.alloc("s4o", 2048)
        rn = sba.alloc("s4n", 2048)
        rs = sba.alloc("s4s", 1024)
        rf = sba.alloc("s4f", 2048)
        xb = rx.mkbufs("s4x", 2)
        ob = r1.mkbufs("s4o", 2)
        nb = rn.mkbufs("s4n", 2)
        fb = rf.mkbufs("s4f", 2)
        for t in range(NT):
            sl = t % 2
            xa = rx.ap(0, 128, sl * 1024, 1024)
            x1 = r1.ap(0, 128, sl * 1024, 1024)
            na = rn.ap(0, 128, sl * 1024, 1024)
            load(xa, dx[t * 128:(t + 1) * 128, :], reads=(W_IN,), writes=(xb[sl],))
            for g in range(2):
                pi = 4 + g
                for j in range(4):
                    kc = g * 4 + j
                    transpose(ps[pi][:, j * 128:(j + 1) * 128], pb[pi],
                              r_acc.ap(0, 128, kc * 2048 + t * 128, 128),
                              reads=(r_acc.buf,), start=(j == 0))
                p.c("v", lambda e, o=x1[:, g * 512:(g + 1) * 512],
                    a=xa[:, g * 512:(g + 1) * 512], b=ps[pi][:, 0:512]:
                    e.tensor_tensor(out=o, in0=a, in1=b, op=ALU.add),
                    reads=(xb[sl], pb[pi]), accs=(ob[sl],))
            store(s_x1[t * 128:(t + 1) * 128, :], x1, reads=(ob[sl],),
                  accs=(dbuf("x1"),))
            if tap == "x1":
                store(d_tap[t * 128:(t + 1) * 128, :], x1, reads=(ob[sl],),
                      accs=(dbuf("tap"),))
            ln_rowwise(x1, ob[sl], na, nb[sl], rs.ap(), rs.buf, D)
            if tap == "y":
                store(d_tap[t * 128:(t + 1) * 128, :], na, reads=(nb[sl],),
                      accs=(dbuf("tap"),))
            tm_to_fm_store(t, na, nb[sl], rf, fb, sl, s_y, dbuf("y"))
        rx.free(); r1.free(); rn.free(); rs.free(); rf.free()

    def stage_R():
        r_wp = sba.alloc("rwp", 1024)
        r_pe = sba.alloc("rpe", 2 * 512)
        r_ev = sba.alloc("rev", 2 * 512)
        wpb = r_wp.mkbuf("rwp")
        peb = r_pe.mkbufs("rpe", 2)
        evb = r_ev.mkbufs("rev", 2)
        for n in range(NC):
            load_panel(r_wp, 0, w_r, D, n, buf=wpb)
            for stripe in range(4):
                q0 = stripe * 512
                for kc in range(NC):
                    sl = kc % 2
                    pa = r_pe.ap(0, 128, sl * 512, 512)
                    load(pa, d_peT[kc * 128:(kc + 1) * 128, q0:q0 + 512],
                         reads=(W_IN,), writes=(peb[sl],))
                    mm(ps[0][:, 0:512], pb[0], r_wp.ap(0, 128, kc * 128, 128), pa,
                       reads=(wpb, peb[sl]), start=(kc == 0), stop=(kc == NC - 1))
                esl = stripe % 2
                ea = r_ev.ap(0, 128, esl * 512, 512)
                copy("v", ea, ps[0][:, 0:512], reads=(pb[0],), writes=(evb[esl],))
                store(s_R[n * 128:(n + 1) * 128, q0:q0 + 512], ea,
                      reads=(evb[esl],), accs=(dbuf("R"),))
        r_wp.free(); r_pe.free(); r_ev.free()

    def stage_xl(split_emit=lambda n: None):
        r_qu = sba.alloc("xqu", 2048)
        r_qv = sba.alloc("xqv", 2048)
        r_k = sba.alloc("xk", 2048)
        r_r = sba.alloc("xr", 2048)
        r_v = sba.alloc("xv", 512)
        r_vs = sba.alloc("xvs", 2 * 16 * 65)
        r_P = sba.alloc("xP", 2048)
        r_PT = sba.alloc("xPT", 4 * 128)
        r_braw = sba.alloc("xbraw", 512)
        r_bskew = sba.alloc("xbskew", 2 * 2048)
        r_ys = sba.alloc("xys", 6 * 512)
        r_wp = sba.alloc("xwp", 3 * 1024)
        r_ub = sba.alloc("xub", 16)
        r_oh = sba.alloc("xoh", 2 * 128)
        r_rz = sba.alloc("xrz", 2 * 128)
        r_of = sba.alloc("xof", 2048)
        qub, qvb, kb = r_qu.mkbuf("xqu"), r_qv.mkbuf("xqv"), r_k.mkbuf("xk")
        rb, vb = r_r.mkbuf("xr"), r_v.mkbuf("xv")
        vsb = r_vs.mkbufs("xvs", 2)
        Pb = r_P.mkbuf("xP")
        PTb = r_PT.mkbufs("xPT", 4)
        brb = r_braw.mkbufs("xbr", 1)
        bsb = r_bskew.mkbufs("xbs", 2)
        ysb = r_ys.mkbufs("xys", 6)
        wpb = r_wp.mkbuf("xwp")
        ubb = r_ub.mkbuf("xub")
        ohb = r_oh.mkbufs("xoh", 2)
        rzb = r_rz.mkbufs("xrz", 2)
        ofb = r_of.mkbuf("xof")
        sbd_b = [dbuf("sbd0"), dbuf("sbd1")]

        load(r_ub.ap(0, 128, 0, 8), d_ub, reads=(W_IN,), writes=(ubb,))
        load(r_ub.ap(0, 128, 8, 8), d_vb, reads=(W_IN,), accs=(ubb,))
        for hp in range(NPAIR):
            split_emit(2)
            load_panel(r_wp, 0, w_qkv, D, hp, buf=wpb)
            load_panel(r_wp, 1024, w_qkv, D, 8 + hp)
            load_panel(r_wp, 2048, w_qkv, D, 16 + hp)
            load(r_r.ap(0, 128, 0, 2048), s_R[hp * 128:(hp + 1) * 128, :],
                 reads=(dbuf("R"),), writes=(rb,))
            for stripe in range(4):
                q0 = stripe * 512
                for kc in range(NC):
                    sl = kc % 6
                    ya = r_ys.ap(0, 128, sl * 512, 512)
                    load(ya, s_y[kc * 128:(kc + 1) * 128, q0:q0 + 512],
                         reads=(dbuf("y"),), writes=(ysb[sl],))
                    for ti in range(3):
                        mm(ps[ti][:, 0:512], pb[ti],
                           r_wp.ap(0, 128, ti * 1024 + kc * 128, 128), ya,
                           reads=(wpb if ti == 0 else r_wp.buf, ysb[sl]),
                           start=(kc == 0), stop=(kc == NC - 1))
                p.c("v", lambda e, o=r_qu.ap(0, 128, q0, 512), i=ps[0][:, 0:512],
                    u=r_ub.ap(0, 128, hp, 1): e.tensor_scalar(
                        out=o, in0=i, scalar1=u, scalar2=None, op0=ALU.add),
                    reads=(pb[0], ubb), accs=(qub,))
                p.c("v", lambda e, o=r_qv.ap(0, 128, q0, 512), i=ps[0][:, 0:512],
                    u=r_ub.ap(0, 128, 8 + hp, 1): e.tensor_scalar(
                        out=o, in0=i, scalar1=u, scalar2=None, op0=ALU.add),
                    reads=(pb[0], ubb), accs=(qvb,))
                copy("v", r_k.ap(0, 128, q0, 512), ps[1][:, 0:512],
                     reads=(pb[1],), accs=(kb,))
                copy("v", r_v.ap(0, 128, 0, 512), ps[2][:, 0:512],
                     reads=(pb[2],), writes=(vb,))
                for j in range(4):
                    kt = stripe * 4 + j
                    for h in range(2):
                        pi2 = 3 + ((j * 2 + h) % 2)
                        transpose(ps[pi2][:, 0:64], pb[pi2],
                                  r_v.ap(h * 64, 64, j * 128, 128), reads=(vb,),
                                  pdim=64, pbase=h * 64)
                        copy("v", r_vs.ap(0, 128, h * 1040 + kt * 65, 64),
                             ps[pi2][:, 0:64], reads=(pb[pi2],), accs=(vsb[h],))
            for h in range(2):
                ones_cols = r_vs.ap(0, 128, h * 1040, 1040).rearrange(
                    "pp (kt c) -> pp kt c", c=65)[:, :, 64:65]
                p.c("v", lambda e, a=ones_cols: e.memset(a, 1.0), accs=(vsb[h],))

            steps = [(h, qt) for h in range(2) for qt in range(NT)]

            def bd_chain2(step_i):
                h, qt = steps[step_i]
                slot = step_i % 2
                W = (qt + 1) * 128
                nch = (W + 511) // 512
                first = True
                for c in range(nch):
                    wc = min(512, W - c * 512)
                    pi2 = (c % 2)
                    mm(ps[pi2][:, 0:wc], pb[pi2],
                       r_qv.ap(h * 64, 64, qt * 128, 128),
                       r_r.ap(h * 64, 64, S - W + c * 512, wc),
                       reads=(qvb, rb), start=True, stop=True)
                    copy("v", r_braw.ap(0, 128, 0, wc),
                         ps[pi2][:, 0:wc], reads=(pb[pi2],), writes=(brb[0],))
                    dst = bass.AP(s_bd.tensor, s_bd.offset + slot * (128 * S) + c * 512,
                                  [[W, 128], [1, wc]])
                    store(dst, r_braw.ap(0, 128, 0, wc),
                          reads=(brb[0],),
                          writes=(sbd_b[slot],) if first else (),
                          accs=(sbd_b[slot],) if not first else ())
                    first = False
                skew = bass.AP(s_bd.tensor, s_bd.offset + slot * (128 * S) + 127,
                               [[W - 1, 128], [1, W]])
                load(r_bskew.ap(0, 128, slot * 2048, W), skew,
                     reads=(sbd_b[slot],), writes=(bsb[slot],), eng="g")

            bd_chain2(0)
            for i, (h, qt) in enumerate(steps):
                split_emit(1)
                slot = i % 2
                W = (qt + 1) * 128
                nch = (W + 511) // 512
                for c in range(nch):
                    wc = min(512, W - c * 512)
                    pi2 = 2 + c
                    mm(ps[pi2][:, 0:wc], pb[pi2],
                       r_qu.ap(h * 64, 64, qt * 128, 128),
                       r_k.ap(h * 64, 64, c * 512, wc),
                       reads=(qub, kb), start=True, stop=True)
                if i + 1 < len(steps):
                    bd_chain2(i + 1)
                for c in range(nch):
                    wc = min(512, W - c * 512)
                    pi2 = 2 + c
                    p.c("v", lambda e, o=r_P.ap(0, 128, c * 512, wc),
                        a=ps[pi2][:, 0:wc],
                        b=r_bskew.ap(0, 128, slot * 2048 + c * 512, wc):
                        e.tensor_tensor(out=o, in0=a, in1=b, op=ALU.add),
                        reads=(pb[pi2], bsb[slot]),
                        writes=(Pb,) if c == 0 else (), accs=(Pb,) if c > 0 else ())
                    p.c("s", lambda e, o=r_P.ap(0, 128, c * 512, wc),
                        sc=C_SXL, b=C_NXL:
                        e.activation(o, o, AF.Exp, scale=sc, bias=b),
                        reads=(r_cimm.buf,), accs=(Pb,))
                p.c("v", lambda e, o=r_P.ap(0, 128, qt * 128, 128),
                    m=r_tri.ap(0, 128, 0, 128):
                    e.tensor_tensor(out=o, in0=o, in1=m, op=ALU.mult),
                    reads=(r_tri.buf,), accs=(Pb,))

                for kt in range(qt + 1):
                    pi2 = 4 + (kt % 2)
                    ptsl = kt % 4
                    transpose(ps[pi2][:, 0:128], pb[pi2],
                              r_P.ap(0, 128, kt * 128, 128), reads=(Pb,))
                    copy("v", r_PT.ap(0, 128, ptsl * 128, 128), ps[pi2][:, 0:128],
                         reads=(pb[pi2],), writes=(PTb[ptsl],))
                    mm(ps[6][0:65, 0:128], pb[6],
                       r_vs.ap(0, 128, h * 1040 + kt * 65, 65),
                       r_PT.ap(0, 128, ptsl * 128, 128),
                       reads=(vsb[h], PTb[ptsl]), start=(kt == 0), stop=(kt == qt))
                oh = r_oh.ap(0, 65, h * 128, 128)
                copy("v", oh, ps[6][0:65, 0:128], reads=(pb[6],), writes=(ohb[h],))

                rz1 = r_rz.ap(64, 1, h * 128, 128)
                p.c("v", lambda e, o=rz1, i=r_oh.ap(64, 1, h * 128, 128):
                    e.reciprocal(o, i), reads=(ohb[h],), writes=(rzb[h],))
                mm(ps[7][0:64, 0:128], pb[7], r_ones.ap(64, 1, 0, 64), rz1,
                   reads=(r_ones.buf, rzb[h]), start=True, stop=True)
                p.c("v", lambda e, o=r_of.ap(h * 64, 64, qt * 128, 128),
                    a=r_oh.ap(0, 64, h * 128, 128), b=ps[7][0:64, 0:128]:
                    e.tensor_tensor(out=o, in0=a, in1=b, op=ALU.mult),
                    reads=(ohb[h], pb[7]), accs=(ofb,))

            wo_accumulate(hp, r_of, ofb, w_fo, first_pair=(hp == 0))
        for r in (r_qu, r_qv, r_k, r_r, r_v, r_vs, r_P, r_PT, r_braw, r_bskew,
                  r_ys, r_wp, r_ub, r_oh, r_rz, r_of):
            r.free()

    def stage_h1():
        r_gb = replicate_ln_params(0, 1)
        ry = sba.alloc("s7y", 1024)
        rt = sba.alloc("s7t", 2048)
        rn = sba.alloc("s7n", 2048)
        rs = sba.alloc("s7s", 1024)
        rf = sba.alloc("s7f", 2048)
        rbf = sba.alloc("s7bf", 2048)
        yb = ry.mkbufs("s7y", 4)
        tb = rt.mkbufs("s7t", 2)
        nb = rn.mkbufs("s7n", 2)
        fb = rf.mkbufs("s7f", 2)
        bfb = rbf.mkbufs("s7bf", 2)
        for t in range(NT):
            sl = t % 2
            ta = rt.ap(0, 128, sl * 1024, 1024)
            na = rn.ap(0, 128, sl * 1024, 1024)

            def get_block(kc, t=t):
                ysl = kc % 4
                ya = ry.ap(0, 128, ysl * 128, 128)
                load(ya, s_y[kc * 128:(kc + 1) * 128, t * 128:(t + 1) * 128],
                     reads=(dbuf("y"),), writes=(yb[ysl],))
                p.c("v", lambda e, o=ya, a=r_acc.ap(0, 128, kc * 2048 + t * 128, 128):
                    e.tensor_tensor(out=o, in0=o, in1=a, op=ALU.add),
                    reads=(r_acc.buf,), accs=(yb[ysl],))
                return ya, (yb[ysl],)

            fm_to_tm_tile(t, get_block, ta, tb[sl])
            if tap == "h1p":
                store(d_tap[t * 128:(t + 1) * 128, :], ta, reads=(tb[sl],),
                      accs=(dbuf("tap"),))
            ln_rowwise(ta, tb[sl], na, nb[sl], rs.ap(), rs.buf, D,
                       gb=(r_gb.ap(0, 128, 0, 1024), r_gb.ap(0, 128, 1024, 1024),
                           r_gb.buf))
            if tap == "h1":
                store(d_tap[t * 128:(t + 1) * 128, :], na, reads=(nb[sl],),
                      accs=(dbuf("tap"),))
            tm_to_fm_store(t, na, nb[sl], rf, fb, sl, s_h1, dbuf("h1"),
                           bf_pair=(s_h1h, s_h1l), rbf=rbf, bfb=bfb)
        ry.free(); rt.free(); rn.free(); rs.free(); rf.free(); rbf.free()
        r_gb.free()

    def split_bf16(hi_ap, lo_ap, src_ap, src_buf, hi_buf, lo_buf, eng1="s",
                   eng2="v"):
        if eng1 == "s":
            p.c("s", lambda e, o=hi_ap, i=src_ap: e.activation(o, i, AF.Copy),
                reads=(src_buf,), accs=(hi_buf,))
        else:
            p.c(eng1, lambda e, o=hi_ap, i=src_ap: e.tensor_copy(out=o, in_=i),
                reads=(src_buf,), accs=(hi_buf,))
        p.c(eng2, lambda e, o=lo_ap, a=src_ap, b=hi_ap:
            e.tensor_tensor(out=o, in0=a, in1=b, op=ALU.subtract),
            reads=(src_buf, hi_buf), accs=(lo_buf,))

    def ffn_expert(h_dram, h_hi_dram, h_lo_dram, h_dbuf, w1h, w1l, w2h, w2l,
                   wsrc_dbuf, get_i1, get_i2, be1_src, be2_src, out_hook, name,
                   need_h32):
        r_h = sba.alloc(f"{name}h", 8 * 512) if need_h32 else None
        r_hx = sba.alloc(f"{name}hx", 8 * 512)
        r_hid = sba.alloc(f"{name}hid", 32 * 512)
        r_hs = sba.alloc(f"{name}hs", 2 * 512)
        r_w1 = sba.alloc(f"{name}w1", 2 * 1024)
        r_w2 = sba.alloc(f"{name}w2", 2 * 2048)
        r_b = sba.alloc(f"{name}b", 40)
        hb = r_h.mkbuf("fh") if need_h32 else None
        hxb = r_hx.mkbuf("fhx")
        hidb = r_hid.mkbufs("fhid", 32)
        hsb = r_hs.mkbufs("fhs", 2)
        w1b = r_w1.mkbufs("fw1", 2)
        w2b = r_w2.mkbufs("fw2", 2)
        bb = r_b.mkbuf("fb")

        def hx_hi(kc, w=512):
            return r_hx.ap(0, 128, kc * 256, w // 2).bitcast(mybir.dt.bfloat16)

        def hx_lo(kc, w=512):
            return r_hx.ap(0, 128, 2048 + kc * 256, w // 2).bitcast(mybir.dt.bfloat16)

        def hid_hi(n):
            return r_hid.ap(0, 128, n * 256, 256).bitcast(mybir.dt.bfloat16)

        def hid_lo(n):
            return r_hid.ap(0, 128, 8192 + n * 256, 256).bitcast(mybir.dt.bfloat16)

        load(r_b.ap(0, 128, 0, 32), be1_src, reads=(W_IN,), writes=(bb,), eng="g")
        load(r_b.ap(0, 128, 32, 8), be2_src, reads=(W_IN,), accs=(bb,), eng="g")
        for stripe in range(4):
            q0 = stripe * 512
            for kc in range(NC):
                if need_h32:
                    load(r_h.ap(0, 128, kc * 512, 512),
                         h_dram[kc * 128:(kc + 1) * 128, q0:q0 + 512],
                         reads=(h_dbuf,), writes=(hb,) if kc == 0 else (),
                         accs=(hb,) if kc > 0 else ())
                load(hx_hi(kc), h_hi_dram[kc * 128:(kc + 1) * 128, q0:q0 + 512],
                     reads=(h_dbuf,), writes=(hxb,) if kc == 0 else (),
                     accs=(hxb,) if kc > 0 else ())
                load(hx_lo(kc), h_lo_dram[kc * 128:(kc + 1) * 128, q0:q0 + 512],
                     reads=(h_dbuf,), accs=(hxb,))
            for n in range(FF // 128):
                wsl = n % 2
                i1 = get_i1(n)
                load_panel_bf(r_w1, wsl * 1024, w1h, D, i1, buf=w1b[wsl])
                load_panel_bf(r_w1, wsl * 1024 + 512, w1l, D, i1, buf=w1b[wsl])
                pi = n % 2
                first = True
                for kc in range(NC):
                    whi = r_w1.ap(0, 128, wsl * 1024 + kc * 64, 64).bitcast(
                        mybir.dt.bfloat16)
                    wlo = r_w1.ap(0, 128, wsl * 1024 + 512 + kc * 64, 64).bitcast(
                        mybir.dt.bfloat16)
                    mm(ps[pi][:, 0:512], pb[pi], whi, hx_hi(kc),
                       reads=(w1b[wsl], hxb), start=first, stop=False)
                    first = False
                    mm(ps[pi][:, 0:512], pb[pi], whi, hx_lo(kc),
                       reads=(w1b[wsl], hxb), start=False, stop=False)
                    mm(ps[pi][:, 0:512], pb[pi], wlo, hx_hi(kc),
                       reads=(w1b[wsl], hxb), start=False, stop=(kc == NC - 1))
                ssl = n % 2
                scr = r_hs.ap(0, 128, ssl * 512, 512)
                p.c("s", lambda e, o=scr, i=ps[pi][:, 0:512],
                    b1=r_b.ap(0, 128, n, 1): e.activation(o, i, AF.Relu, bias=b1),
                    reads=(pb[pi], bb), writes=(hsb[ssl],))
                p.c("v", lambda e, o=hid_hi(n), i=scr: e.tensor_copy(out=o, in_=i),
                    reads=(hsb[ssl],), writes=(hidb[n],))
                p.c("v", lambda e, o=hid_lo(n), a=scr, b=hid_hi(n):
                    e.tensor_tensor(out=o, in0=a, in1=b, op=ALU.subtract),
                    reads=(hsb[ssl],), accs=(hidb[n],))
            for m in range(NC):
                pi = 2 + (m % 2)
                first = True
                for half in range(2):
                    wsl = half
                    i2 = get_i2(m)
                    load_panel_bf(r_w2, wsl * 2048, w2h, FF, i2, buf=w2b[wsl],
                                  k0=half * 2048, kn=2048, eng="g")
                    load_panel_bf(r_w2, wsl * 2048 + 1024, w2l, FF, i2,
                                  buf=w2b[wsl], k0=half * 2048, kn=2048, eng="g")
                    for j in range(16):
                        n = half * 16 + j
                        whi = r_w2.ap(0, 128, wsl * 2048 + j * 64, 64).bitcast(
                            mybir.dt.bfloat16)
                        wlo = r_w2.ap(0, 128, wsl * 2048 + 1024 + j * 64, 64).bitcast(
                            mybir.dt.bfloat16)
                        mm(ps[pi][:, 0:512], pb[pi], whi, hid_hi(n),
                           reads=(w2b[wsl], hidb[n]), start=first, stop=False)
                        first = False
                        mm(ps[pi][:, 0:512], pb[pi], whi, hid_lo(n),
                           reads=(w2b[wsl], hidb[n]), start=False, stop=False)
                        mm(ps[pi][:, 0:512], pb[pi], wlo, hid_hi(n),
                           reads=(w2b[wsl], hidb[n]), start=False,
                           stop=(n == FF // 128 - 1))
                out_hook(m, stripe, ps[pi][:, 0:512], pb[pi], r_b, bb, r_h, hb)
        if need_h32:
            r_h.free()
        r_hx.free(); r_hid.free(); r_hs.free()
        r_w1.free(); r_w2.free(); r_b.free()

    def stage_h2():
        r_gb = replicate_ln_params(2, 3)
        rh2 = sba.alloc("s9h", 1024)
        rt = sba.alloc("s9t", 2048)
        rn = sba.alloc("s9n", 2048)
        rs = sba.alloc("s9s", 1024)
        rf = sba.alloc("s9f", 2048)
        rbf = sba.alloc("s9bf", 2048)
        h2b = rh2.mkbufs("s9h", 4)
        tb = rt.mkbufs("s9t", 2)
        nb = rn.mkbufs("s9n", 2)
        fb = rf.mkbufs("s9f", 2)
        bfb = rbf.mkbufs("s9bf", 2)
        for t in range(NT):
            sl = t % 2
            ta = rt.ap(0, 128, sl * 1024, 1024)
            na = rn.ap(0, 128, sl * 1024, 1024)

            def get_block(kc, t=t):
                ysl = kc % 4
                ha = rh2.ap(0, 128, ysl * 128, 128)
                load(ha, s_h2p[kc * 128:(kc + 1) * 128, t * 128:(t + 1) * 128],
                     reads=(dbuf("h2p"),), writes=(h2b[ysl],))
                return (ha, (h2b[ysl],))

            fm_to_tm_tile(t, get_block, ta, tb[sl])
            ln_rowwise(ta, tb[sl], na, nb[sl], rs.ap(), rs.buf, D,
                       gb=(r_gb.ap(0, 128, 0, 1024), r_gb.ap(0, 128, 1024, 1024),
                           r_gb.buf))
            if tap == "h2":
                store(d_tap[t * 128:(t + 1) * 128, :], na, reads=(nb[sl],),
                      accs=(dbuf("tap"),))
            tm_to_fm_store(t, na, nb[sl], rf, fb, sl, s_h2, dbuf("h2"),
                           bf_pair=(s_h2h, s_h2l), rbf=rbf, bfb=bfb)
        rh2.free(); rt.free(); rn.free(); rs.free(); rf.free(); rbf.free()
        r_gb.free()

    def stage_gating(r_dg):
        r_wg = sba.alloc("gwg", 64)
        r_lg = sba.alloc("glg", 2048)
        r_hs = sba.alloc("ghs", 2 * 512)
        r_tm = sba.alloc("gtm", 24)
        wgb, lgb = r_wg.mkbuf("gwg"), r_lg.mkbuf("glg")
        hsb = r_hs.mkbufs("ghs", 2)
        tmb = r_tm.mkbuf("gtm")
        load(r_wg.ap(0, 128, 0, 64).rearrange("pp (c n) -> pp c n", n=8),
             w_g.rearrange("(c pp) n -> pp c n", pp=128), reads=(W_IN,),
             writes=(wgb,))
        for stripe in range(4):
            q0 = stripe * 512
            for kc in range(NC):
                sl = kc % 2
                ha = r_hs.ap(0, 128, sl * 512, 512)
                load(ha, s_h2[kc * 128:(kc + 1) * 128, q0:q0 + 512],
                     reads=(dbuf("h2"),), writes=(hsb[sl],))
                mm(ps[0][0:8, 0:512], pb[0], r_wg.ap(0, 128, kc * 8, 8), ha,
                   reads=(wgb, hsb[sl]), start=(kc == 0), stop=(kc == NC - 1))
            copy("v", r_lg.ap(0, 8, q0, 512), ps[0][0:8, 0:512],
                 reads=(pb[0],), accs=(lgb,))
        st = r_stats
        for t in range(NT):
            transpose(ps[1][:, 0:8], pb[1], r_lg.ap(0, 8, t * 128, 128),
                      reads=(lgb,), pdim=8)
            la = r_tm.ap(0, 128, 0, 8)
            copy("v", la, ps[1][:, 0:8], reads=(pb[1],), writes=(tmb,))
            mx = st.ap(0, 128, 8, 1)
            z = st.ap(0, 128, 9, 1)
            gt = st.ap(0, 128, 10, 1)
            nmx = st.ap(0, 128, 11, 1)
            p.c("v", lambda e, o=mx, i=la: e.reduce_max(o, i, axis=AX),
                reads=(tmb,), accs=(st.buf,))
            p.c("v", lambda e, o=nmx, i=mx: e.tensor_scalar(
                out=o, in0=i, scalar1=-1.0, scalar2=None, op0=ALU.mult),
                accs=(st.buf,))
            p.c("s", lambda e, o=r_tm.ap(0, 128, 8, 8), i=la, b=nmx, a=z:
                e.activation(o, i, AF.Exp, bias=b, accum_out=a),
                reads=(tmb,), accs=(tmb, st.buf))
            p.c("v", lambda e, o=gt, i=z: e.reciprocal(o, i), accs=(st.buf,))
            p.c("v", lambda e, o=r_tm.ap(0, 128, 16, 8), i=la, a=mx, b=gt:
                e.tensor_scalar(out=o, in0=i, scalar1=a, scalar2=b,
                                op0=ALU.is_equal, op1=ALU.mult),
                reads=(tmb,), accs=(tmb,))
            transpose(ps[2][0:8, 0:128], pb[2], r_tm.ap(0, 128, 16, 8),
                      reads=(tmb,))
            copy("v", r_dg.ap(0, 8, t * 128, 128), ps[2][0:8, 0:128],
                 reads=(pb[2],), accs=(r_dg.buf,))
        r_wg.free(); r_lg.free(); r_hs.free(); r_tm.free()

    def stage_h3(r_h3):
        r_gb = replicate_ln_params(4, 5)
        rh = sba.alloc("s11h", 1024)
        rm3 = sba.alloc("s11m", 1024)
        rt = sba.alloc("s11t", 2048)
        rn = sba.alloc("s11n", 2048)
        rs = sba.alloc("s11s", 1024)
        hbk = rh.mkbufs("s11h", 4)
        mbk = rm3.mkbufs("s11m", 4)
        tb = rt.mkbufs("s11t", 2)
        nb = rn.mkbufs("s11n", 2)
        for t in range(NT):
            sl = t % 2
            ta = rt.ap(0, 128, sl * 1024, 1024)
            na = rn.ap(0, 128, sl * 1024, 1024)

            def get_block(kc, t=t):
                ysl = kc % 4
                ha = rh.ap(0, 128, ysl * 128, 128)
                ma = rm3.ap(0, 128, ysl * 128, 128)
                load(ha, s_h2[kc * 128:(kc + 1) * 128, t * 128:(t + 1) * 128],
                     reads=(dbuf("h2"),), writes=(hbk[ysl],))
                load(ma, s_moe[kc * 128:(kc + 1) * 128, t * 128:(t + 1) * 128],
                     reads=(dbuf("moe"),), writes=(mbk[ysl],), eng="g")
                p.c("v", lambda e, o=ha, a=ma:
                    e.tensor_tensor(out=o, in0=o, in1=a, op=ALU.add),
                    reads=(mbk[ysl],), accs=(hbk[ysl],))
                return ha, (hbk[ysl],)

            fm_to_tm_tile(t, get_block, ta, tb[sl])
            ln_rowwise(ta, tb[sl], na, nb[sl], rs.ap(), rs.buf, D,
                       gb=(r_gb.ap(0, 128, 0, 1024), r_gb.ap(0, 128, 1024, 1024),
                           r_gb.buf))
            if tap == "h3":
                store(d_tap[t * 128:(t + 1) * 128, :], na, reads=(nb[sl],),
                      accs=(dbuf("tap"),))
            for g in range(2):
                pi = 6 + g
                for j in range(4):
                    kc = g * 4 + j
                    transpose(ps[pi][:, j * 128:(j + 1) * 128], pb[pi],
                              na[:, kc * 128:(kc + 1) * 128], reads=(nb[sl],),
                              start=(j == 0))
                for j in range(4):
                    kc = g * 4 + j
                    copy("v", r_h3.ap(0, 128, kc * 2048 + t * 128, 128),
                         ps[pi][:, j * 128:(j + 1) * 128], reads=(pb[pi],),
                         accs=(r_h3.buf,))
        rh.free(); rm3.free(); rt.free(); rn.free(); rs.free(); r_gb.free()

    def stage_out(r_h3):
        r_wp = sba.alloc("owp", 2 * 1024)
        r_fm = sba.alloc("ofm", 2 * 512)
        r_x1 = sba.alloc("ox1", 4 * 128)
        r_ob = sba.alloc("oob", 4 * 128)
        wpb = r_wp.mkbufs("ow", 2)
        fmb = r_fm.mkbufs("of", 2)
        x1b = r_x1.mkbufs("ox", 4)
        obb = r_ob.mkbufs("oo", 4)
        for n in range(NC):
            wsl = n % 2
            load_panel(r_wp, wsl * 1024, w_out, D, n, buf=wpb[wsl])
            for stripe in range(4):
                pi = n % 2
                for kc in range(NC):
                    mm(ps[pi][:, 0:512], pb[pi],
                       r_wp.ap(0, 128, wsl * 1024 + kc * 128, 128),
                       r_h3.ap(0, 128, kc * 2048 + stripe * 512, 512),
                       reads=(wpb[wsl], r_h3.buf), start=(kc == 0),
                       stop=(kc == NC - 1))
                fsl = stripe % 2
                fa = r_fm.ap(0, 128, fsl * 512, 512)
                copy("v", fa, ps[pi][:, 0:512], reads=(pb[pi],), writes=(fmb[fsl],))
                for j in range(4):
                    t0 = stripe * 512 + j * 128
                    pj = 2 + (j % 2)
                    transpose(ps[pj][:, 0:128], pb[pj], fa[:, j * 128:(j + 1) * 128],
                              reads=(fmb[fsl],))
                    xa = r_x1.ap(0, 128, j * 128, 128)
                    load(xa, s_x1[t0:t0 + 128, n * 128:(n + 1) * 128],
                         reads=(dbuf("x1"),), writes=(x1b[j],))
                    oa = r_ob.ap(0, 128, j * 128, 128)
                    p.c("v", lambda e, o=oa, a=xa, b=ps[pj][:, 0:128]:
                        e.tensor_tensor(out=o, in0=a, in1=b, op=ALU.add),
                        reads=(x1b[j], pb[pj]), writes=(obb[j],))
                    store(dout[t0:t0 + 128, n * 128:(n + 1) * 128], oa,
                          reads=(obb[j],), accs=(dbuf("outd"),))
        r_wp.free(); r_fm.free(); r_x1.free(); r_ob.free()

    stage_ln_x()
    if upto in ("h2", "full"):
        split_emit, split_finish = make_split_shards()
    else:
        split_emit, split_finish = (lambda n: None), (lambda: None)
    if upto != "xn":
        r_lat = stage_lat()
        stage_mla(r_lat, split_emit)
        r_lat.free()
        stage_x1_y()
    if upto in ("h1", "h2", "full"):
        stage_R()
        stage_xl(split_emit)
        stage_h1()
    if upto in ("h2", "full"):
        r_acc.free()
        r_fo2 = sba.alloc("ffout", 2 * 512)
        fo2b = r_fo2.mkbufs("ffout", 2)

        def ff_hook(m, stripe, psum_ap, psum_buf, r_b, bb, r_h, hb):
            sl = m % 2
            dst = r_fo2.ap(0, 128, sl * 512, 512)
            p.c("v", lambda e, o=dst, i=psum_ap, b2=r_b.ap(0, 128, 32 + m, 1),
                h1t=r_h.ap(0, 128, m * 512, 512):
                e.scalar_tensor_tensor(out=o, in0=i, scalar=b2, in1=h1t,
                                       op0=ALU.add, op1=ALU.add),
                reads=(psum_buf, bb, hb), writes=(fo2b[sl],))
            store(s_h2p[m * 128:(m + 1) * 128, stripe * 512:stripe * 512 + 512],
                  dst, reads=(fo2b[sl],), accs=(dbuf("h2p"),))

        split_finish()
        ffn_expert(s_h1, s_h1h, s_h1l, dbuf("h1"), s_f1h, s_f1l, s_f2h, s_f2l,
                   dbuf("ws_f1"), lambda n: n, lambda m: m,
                   d_bff1, d_bff2, ff_hook, "ff", need_h32=True)
        r_fo2.free()
        stage_h2()
    if upto == "full":
        r_dg = sba.alloc("dg", 2048)
        stage_gating(r_dg)
        r_rep = sba.alloc("rep", 512)
        r_tmpm = sba.alloc("tmpm", 2 * 512)
        r_macc = sba.alloc("macc", 2 * 512)
        r_sel = sba.alloc("sel", 8 * 128)
        repb = r_rep.mkbuf("rep")
        tmb2 = r_tmpm.mkbufs("tmpm", 2)
        maccb = r_macc.mkbufs("macc", 2)
        selb = r_sel.mkbuf("sel")
        load(r_sel.ap(0, 8, 0, 8 * 128), d_esel, reads=(W_IN,), writes=(selb,))

        for e_idx in range(NE):
            def moe_hook(m, stripe, psum_ap, psum_buf, r_b, bb, r_h, hb,
                         e_idx=e_idx):
                if m == 0:
                    mm(ps[7][:, 0:512], pb[7], r_sel.ap(0, 8, e_idx * 128, 128),
                       r_dg.ap(0, 8, stripe * 512, 512),
                       reads=(selb, r_dg.buf), start=True, stop=True)
                    copy("v", r_rep.ap(0, 128, 0, 512), ps[7][:, 0:512],
                         reads=(pb[7],), writes=(repb,))
                sl = m % 2
                dst = r_tmpm.ap(0, 128, sl * 512, 512)
                dram = s_moe[m * 128:(m + 1) * 128, stripe * 512:stripe * 512 + 512]
                if e_idx == 0:
                    p.c("v", lambda e, o=dst, i=psum_ap,
                        b2=r_b.ap(0, 128, 32 + m, 1), rp=r_rep.ap(0, 128, 0, 512):
                        e.scalar_tensor_tensor(out=o, in0=i, scalar=b2, in1=rp,
                                               op0=ALU.add, op1=ALU.mult),
                        reads=(psum_buf, bb, repb), writes=(tmb2[sl],))
                else:
                    prev = r_macc.ap(0, 128, sl * 512, 512)
                    load(prev, dram, reads=(dbuf("moe"),), writes=(maccb[sl],))
                    p.c("v", lambda e, o=dst, i=psum_ap,
                        b2=r_b.ap(0, 128, 32 + m, 1), rp=r_rep.ap(0, 128, 0, 512):
                        e.scalar_tensor_tensor(out=o, in0=i, scalar=b2, in1=rp,
                                               op0=ALU.add, op1=ALU.mult),
                        reads=(psum_buf, bb, repb), writes=(tmb2[sl],))
                    p.c("v", lambda e, o=dst, i=prev:
                        e.tensor_tensor(out=o, in0=o, in1=i, op=ALU.add),
                        reads=(maccb[sl],), accs=(tmb2[sl],))
                store(dram, dst, reads=(tmb2[sl],), accs=(dbuf("moe"),))

            ffn_expert(s_h2, s_h2h, s_h2l, dbuf("h2"), s_e1h, s_e1l,
                       s_e2h, s_e2l, dbuf(f"ws_e1_{e_idx}"),
                       lambda n, e_idx=e_idx: (e_idx, n),
                       lambda m, e_idx=e_idx: (e_idx, m),
                       d_be1[e_idx], d_be2[e_idx], moe_hook, f"moe{e_idx}",
                       need_h32=False)
        r_dg.free(); r_rep.free(); r_tmpm.free(); r_macc.free(); r_sel.free()

        r_h3 = sba.alloc("h3", 16384)
        stage_h3(r_h3)
        stage_out(r_h3)
        r_h3.free()

    if upto == "full":
        p.c("g", None, reads=(dbuf("outd"),))
    if tap is not None:
        p.c("g", None, reads=(dbuf("tap"),))

    sem_handles = {}
    import contextlib
    stack = contextlib.ExitStack()
    keys = set()
    for o in p.ops:
        keys.add(o.dcls if o.kind == "d" else (o.eng, "c"))
    for key in sorted(keys, key=str):
        sem_handles[key] = stack.enter_context(
            nc.semaphore(f"sem_{key[0]}_{key[1]}"))
    with nc.Block() as block:
        p.finalize_and_emit(nc, block, sem_handles)
    stack.close()
    return nc


def kernel(**inputs):
    gi = inputs

    def P(name):
        return _panels(gi[name])

    decay, peT, ident, tri, esel = _host_consts()
    shared = {
        "w_q": P("L0_attn_Wq"), "w_k": P("L0_attn_Wk"), "w_v": P("L0_attn_Wv"),
        "w_o": P("L0_attn_Wo"), "w_dn": P("L0_attn_W_down_kv"),
        "w_uk": P("L0_attn_W_up_k"), "w_uv": P("L0_attn_W_up_v"),
        "w_qkv": P("L0_ffn_Wqkv"), "w_fo": P("L0_ffn_Wo"), "w_r": P("L0_ffn_Wr"),
        "u_bias": np.ascontiguousarray(np.asarray(
            gi["L0_ffn_u_bias"], np.float32).reshape(8, 128).T),
        "v_bias": np.ascontiguousarray(np.asarray(
            gi["L0_ffn_v_bias"], np.float32).reshape(8, 128).T),
        "ln_gb": np.ascontiguousarray(np.stack([
            gi["L0_ffn_ln1_g"], gi["L0_ffn_ln1_b"], gi["L0_ffn_ln2_g"],
            gi["L0_ffn_ln2_b"], gi["L0_ffn_ln3_g"], gi["L0_ffn_ln3_b"]]), np.float32),
        "w_ff1": P("L0_ffn_Wff1"),
        "b_ff1": np.ascontiguousarray(np.asarray(
            gi["L0_ffn_bff1"], np.float32).reshape(32, 128).T),
        "w_ff2": P("L0_ffn_Wff2"),
        "b_ff2": np.ascontiguousarray(np.asarray(
            gi["L0_ffn_bff2"], np.float32).reshape(8, 128).T),
        "w_g": np.ascontiguousarray(gi["L0_ffn_Wg"], np.float32),
        "w_e1": np.stack([_panels(gi["L0_ffn_We1"][e]) for e in range(NE)]),
        "b_e1": np.ascontiguousarray(np.asarray(
            gi["L0_ffn_be1"], np.float32).reshape(NE, 32, 128).transpose(0, 2, 1)),
        "w_e2": np.stack([_panels(gi["L0_ffn_We2"][e]) for e in range(NE)]),
        "b_e2": np.ascontiguousarray(np.asarray(
            gi["L0_ffn_be2"], np.float32).reshape(NE, 8, 128).transpose(0, 2, 1)),
        "w_out": P("L0_ffn_Wout"),
        "c_decay": decay, "c_peT": peT, "c_ident": ident, "c_tri": tri,
        "c_esel": esel,
    }
    x_full = np.ascontiguousarray(np.asarray(gi["L0_attn_x"], np.float32))
    nc = build_program(tap=None, upto="full")
    in_maps = [dict(shared, x=x_full[i]) for i in range(B)]
    res = run_bass_kernel_spmd(nc, in_maps, core_ids=list(range(B)))
    return np.stack([r["out"] for r in res.results], axis=0)


# version 09
# speedup vs baseline: 2.1451x; 1.1222x over previous
import math
import numpy as np

import concourse.bass as bass
import concourse.mybir as mybir
from concourse.bass_utils import run_bass_kernel_spmd

F32 = mybir.dt.float32
AF = mybir.ActivationFunctionType
ALU = mybir.AluOpType
AX = mybir.AxisListType.X

B, S, D = 8, 2048, 1024
NH, DH = 16, 64
DL = 256
FF, NE = 4096, 8
NT = S // 128
NC = D // 128
NPAIR = NH // 2
SCALE_MLA = 1.0 / (math.sqrt(1024.0) + 1e-8)
SCALE_XL = 0.125
CLAMP = 10000.0
C_MLA = 1.0
C_XL = 8.0
LN_EPS = 1e-5
SB_CAP = 47000
HAZ_GAP = 3


class _Buf:
    __slots__ = ("w", "r", "name", "dsem", "gsrc", "touched")

    def __init__(self, name=""):
        self.w = []
        self.r = []
        self.name = name
        self.dsem = None
        self.gsrc = []
        self.touched = False


class _Op:
    __slots__ = ("i", "eng", "kind", "fn", "deps", "inc", "val", "waits",
                 "dcls", "superseded", "dbg")


_DMA_CLASSES = {"q": 44, "g": 32, "s": 14, "v": 2, "t": 2}


class Prog:
    def __init__(self):
        self.ops = []
        self.dcls_state = {}
        self.dcls_free = {e: list(range(n)) for e, n in _DMA_CLASSES.items()}

    def release_class(self, buf):
        if buf.dsem is not None:
            self.dcls_free[buf.dsem[0]].append(buf.dsem[1])

    def _mk(self, eng, kind, fn, reads, writes, accs):
        o = _Op()
        o.i = len(self.ops)
        o.eng = eng
        o.kind = kind
        o.fn = fn
        o.inc = False
        o.val = 0
        o.waits = []
        seen = {o.i}
        deps = []

        def add(d):
            if d.i not in seen:
                seen.add(d.i)
                deps.append(d)

        import os as _os
        _dbg_target = _os.environ.get("KDBG_DEP")

        def add2(d, b, why):
            if _dbg_target is not None and d.i == int(_dbg_target):
                print(f"[dep-forensic] op{len(self.ops)} gets dep op{d.i} "
                      f"via buf {b.name!r} ({why})")
            add(d)

        def ghost(b):
            b.touched = True
            if b.gsrc:
                if _os.environ.get("KDBG_GHOST") and b.name == "sel":
                    print(f"[ghost] op{len(self.ops)} buf=sel gsrc="
                          f"{[g.name for g in b.gsrc]}")
                for gb in b.gsrc:
                    gb.r.append(o)
                b.gsrc = []
            elif _os.environ.get("KDBG_GHOST") and b.name == "sel":
                print(f"[ghost] op{len(self.ops)} buf=sel gsrc=EMPTY")

        for b in reads:
            ghost(b)
            for w in b.w:
                add2(w, b, "read<-w")
            b.r.append(o)
        for b in writes:
            ghost(b)
            for w in b.w:
                add2(w, b, "write<-w")
            for r in b.r:
                add2(r, b, "write<-r")
            b.w = [o]
            b.r = []
        for b in accs:
            ghost(b)
            for w in b.w:
                add2(w, b, "acc<-w")
            for r in b.r:
                add2(r, b, "acc<-r")
            b.w = [o]
            b.r = []
        o.deps = deps
        o.dcls = None
        o.superseded = None
        o.dbg = ""
        if kind == "d":
            wbufs = list(writes) + list(accs)
            assert len(wbufs) == 1, f"dma must write exactly one buf ({eng})"
            b = wbufs[0]
            if b.dsem is not None and b.dsem[0] != eng:
                b.dsem = None
            if b.dsem is None:
                free = self.dcls_free[eng]
                if not free:
                    raise RuntimeError(f"out of dma sem classes for queue {eng}")
                ci = free.pop(0)
                b.dsem = (eng, ci)
                st = self.dcls_state.get(b.dsem)
                if st is not None and st["op"] is not None:
                    import os as _os2
                    if _os2.environ.get("KDBG_SUP"):
                        print(f"[takeover] op{len(self.ops)} class {b.dsem} "
                              f"from {[ (ob.name, [x.i for x in ob.w][:5], [x.i for x in ob.r][:50]) for ob in st['bufs'] ]}")
                    add(st["op"])
                    for ob in st["bufs"]:
                        for x in ob.w + ob.r:
                            add(x)
                    st["op"].superseded = o.i
                self.dcls_state[b.dsem] = {"op": o, "bufs": [b]}
            else:
                st = self.dcls_state[b.dsem]
                if st["bufs"] and st["bufs"][0] is not b:
                    if st["op"] is not None:
                        add(st["op"])
                        for ob in st["bufs"]:
                            for x in ob.w + ob.r:
                                add(x)
                        st["op"].superseded = o.i
                st["op"] = o
                st["bufs"] = [b]
            o.dcls = b.dsem
            o.dbg = b.name
        self.ops.append(o)
        return o

    def c(self, eng, fn, reads=(), writes=(), accs=()):
        return self._mk(eng, "c", fn, reads, writes, accs)

    def d(self, eng, fn, reads=(), writes=(), accs=()):
        return self._mk(eng, "d", fn, reads, writes, accs)

    def finalize_and_emit(self, nc, block, sem_of):
        pos = {}
        tmp = {}
        for o in self.ops:
            lst = tmp.setdefault(o.eng, [])
            pos[o.i] = len(lst)
            lst.append(o)

        def covered(o, dp):
            if dp.eng != o.eng or dp.kind != "c" or o.kind != "c":
                return False
            if o.eng == "t":
                return True
            return pos[o.i] - pos[dp.i] > HAZ_GAP

        def semkey(o):
            return o.dcls if o.kind == "d" else (o.eng, "c")

        for o in self.ops:
            if o.kind == "d":
                o.inc = True
            kmax = {}
            for dp in o.deps:
                if covered(o, dp):
                    continue
                key = (dp.eng, dp.kind)
                cur = kmax.get(key)
                if cur is None or dp.i > cur.i:
                    kmax[key] = dp
            for dp in kmax.values():
                dp.inc = True
        cnt = {}
        for o in self.ops:
            if o.inc:
                key = semkey(o)
                cnt[key] = cnt.get(key, 0) + (16 if o.kind == "d" else 1)
                o.val = cnt[key]
        for o in self.ops:
            wmax = {}
            for dp in o.deps:
                while (dp.kind == "d" and dp.superseded is not None
                       and dp.superseded < o.i):
                    dp = self.ops[dp.superseded]
                ok = True
                if dp.kind == "d" and dp.superseded is not None:
                    if dp.superseded == o.i:
                        pass
                    else:
                        sup = self.ops[dp.superseded]
                        ok = False
                        seen2 = set()
                        stk2 = [d for d in sup.deps if d.i >= o.i]
                        while stk2:
                            d = stk2.pop()
                            if d.i in seen2:
                                continue
                            seen2.add(d.i)
                            if d is o or d.eng == o.eng:
                                ok = True
                                break
                            stk2.extend(y for y in d.deps
                                        if y.i >= o.i and y.i not in seen2)
                        if not ok:
                            target = dp
                            visited = set()
                            stack = [x for x in o.deps
                                     if x is not dp and x.i > target.i]
                            while stack:
                                x = stack.pop()
                                if x.i in visited:
                                    continue
                                visited.add(x.i)
                                if x is target:
                                    ok = True
                                    break
                                for y in x.deps:
                                    if y is target:
                                        ok = True
                                        break
                                    if y.i > target.i and y.i not in visited:
                                        stack.append(y)
                                if ok:
                                    break
                            if ok:
                                continue
                if not ok:
                    raise RuntimeError(
                        f"unsound dma wait: op{o.i}({o.eng}) depends on "
                        f"superseded dma op{dp.i}->buf {dp.dbg!r} (superseded "
                        f"at op{dp.superseded}->buf "
                        f"{self.ops[dp.superseded].dbg!r}, class {dp.dcls})")
                if covered(o, dp):
                    continue
                key = semkey(dp)
                if wmax.get(key, 0) < dp.val:
                    wmax[key] = dp.val
            o.waits = sorted(wmax.items(), key=lambda kv: str(kv[0]))

        by_eng = {}
        for o in self.ops:
            by_eng.setdefault(o.eng, []).append(o)

        def body_for(name):
            ops = by_eng.get(name, [])

            def body(eng):
                hw = {}
                for o in ops:
                    for key, val in o.waits:
                        if hw.get(key, 0) < val:
                            eng.wait_ge(sem_of[key], val)
                            hw[key] = val
                    inst = o.fn(eng) if o.fn is not None else None
                    if o.inc:
                        inst.then_inc(sem_of[semkey(o)],
                                      16 if o.kind == "d" else 1)
            return body

        block.tensor(body_for("t"))
        block.vector(body_for("v"))
        block.scalar(body_for("s"))
        block.gpsimd(body_for("g"))
        block.sync(body_for("q"))
        return cnt


class SBAlloc:
    def __init__(self, tensor_ap, cap, prog=None):
        self.t = tensor_ap
        self.cap = cap
        self.prog = prog
        self.free_list = [(0, cap)]
        self.ghosts = []
        self.peak = 0
        self.used = 0

    def alloc(self, name, size):
        if not hasattr(self, "live"):
            self.live = {}
        for i, (off, sz) in enumerate(self.free_list):
            if sz >= size:
                if sz == size:
                    self.free_list.pop(i)
                else:
                    self.free_list[i] = (off + size, sz - size)
                buf = _Buf(name)
                seen = set()
                import os as _os3
                _tgt = _os3.environ.get("KDBG_ALLOC")
                for goff, gsz, gbufs in self.ghosts:
                    if goff < off + size and off < goff + gsz:
                        for gbuf in gbufs:
                            if _tgt and name == "sel":
                                ids = [x.i for x in gbuf.w + gbuf.r]
                                if int(_tgt) in ids:
                                    print(f"[alloc-forensic] sel inherits "
                                          f"op{_tgt} from gbuf {gbuf.name!r} "
                                          f"(w={[x.i for x in gbuf.w][:6]})")
                            if gbuf not in buf.gsrc:
                                buf.gsrc.append(gbuf)
                            for x in gbuf.w + gbuf.r:
                                if x.i not in seen:
                                    seen.add(x.i)
                                    buf.w.append(x)
                self.used += size
                self.peak = max(self.peak, self.used)
                self.live[(name, off)] = size
                return Region(self, name, off, size, buf)
        raise RuntimeError(f"SBUF alloc failed: {name} size {size}, "
                           f"free={self.free_list}, live={getattr(self, 'live', {})}")

    def release(self, region):
        self.live.pop((region.name, region.off), None)
        if self.prog is not None:
            for b in [region.buf] + list(region.extra):
                self.prog.release_class(b)
        tb = [b for b in [region.buf] + list(region.extra) if b.touched]
        if tb:
            off, size = region.off, region.size
            self.ghosts = [g for g in self.ghosts
                           if not (off <= g[0] and g[0] + g[1] <= off + size)]
            self.ghosts.append((off, size, tb))
        self.used -= region.size
        self.free_list.append((region.off, region.size))
        self.free_list.sort()
        merged = []
        for off, sz in self.free_list:
            if merged and merged[-1][0] + merged[-1][1] == off:
                merged[-1] = (merged[-1][0], merged[-1][1] + sz)
            else:
                merged.append((off, sz))
        self.free_list = merged


class Region:
    __slots__ = ("sba", "name", "off", "size", "buf", "extra")

    def __init__(self, sba, name, off, size, buf):
        self.sba = sba
        self.name = name
        self.off = off
        self.size = size
        self.buf = buf
        self.extra = []

    def mkbuf(self, name):
        b = _Buf(name)
        b.w = list(self.buf.w)
        b.gsrc = list(self.buf.gsrc)
        self.extra.append(b)
        return b

    def mkbufs(self, name, n):
        return [self.mkbuf(f"{name}{i}") for i in range(n)]

    def ap(self, p0=0, pn=128, c0=0, cn=None):
        if cn is None:
            cn = self.size - c0
        assert c0 + cn <= self.size, (self.name, c0, cn, self.size)
        return self.sba.t[p0:p0 + pn, self.off + c0:self.off + c0 + cn]

    def free(self):
        self.sba.release(self)


def _panels(w):
    k, n = np.asarray(w).shape
    return np.ascontiguousarray(np.asarray(w, np.float32).reshape(
        k, n // 128, 128).transpose(1, 0, 2))


def _host_consts():
    idx = np.arange(S, dtype=np.float64)
    decay = np.exp(-0.01 * 0.01 * np.abs(idx[:, None] - idx[None, :])).astype(np.float32)
    pos = np.arange(S - 1, -1, -1, dtype=np.float64)
    inv_freq = 1.0 / 10000.0 ** (np.arange(0, D, 2, dtype=np.float64) / D)
    sinusoid = pos[:, None] * inv_freq[None, :]
    pe = np.concatenate([np.sin(sinusoid), np.cos(sinusoid)], axis=-1)
    peT = np.ascontiguousarray(pe.T.astype(np.float32))
    ident = np.concatenate([np.eye(128, dtype=np.float32),
                            np.tile(np.eye(64, dtype=np.float32), (2, 1))], axis=1)
    ident = np.ascontiguousarray(ident)
    tri = np.tril(np.ones((128, 128), np.float32))
    esel = np.zeros((8, 8 * 128), np.float32)
    for e in range(8):
        esel[e, e * 128:(e + 1) * 128] = 1.0
    return decay, peT, ident, tri, esel


def build_program(tap=None, upto="full"):
    nc = bass.Bass("TRN2", target_bir_lowering=False, debug=False)

    dx = nc.dram_tensor("x", [S, D], F32, kind="ExternalInput").ap()
    dout = nc.dram_tensor("out", [S, D], F32, kind="ExternalOutput").ap()

    def win(name, shape):
        return nc.dram_tensor(name, list(shape), F32, kind="ExternalInput").ap()

    w_q = win("w_q", (NC, D, 128))
    w_k = win("w_k", (NC, D, 128))
    w_v = win("w_v", (NC, D, 128))
    w_o = win("w_o", (NC, D, 128))
    w_dn = win("w_dn", (2, D, 128))
    w_uk = win("w_uk", (NC, DL, 128))
    w_uv = win("w_uv", (NC, DL, 128))
    w_qkv = win("w_qkv", (24, D, 128))
    w_fo = win("w_fo", (NC, D, 128))
    w_r = win("w_r", (NC, D, 128))
    d_ub = win("u_bias", (128, 8))
    d_vb = win("v_bias", (128, 8))
    d_ln = win("ln_gb", (6, D))
    w_ff1 = win("w_ff1", (FF // 128, D, 128))
    d_bff1 = win("b_ff1", (128, 32))
    w_ff2 = win("w_ff2", (NC, FF, 128))
    d_bff2 = win("b_ff2", (128, 8))
    w_g = win("w_g", (D, NE))
    w_e1 = win("w_e1", (NE, FF // 128, D, 128))
    d_be1 = win("b_e1", (NE, 128, 32))
    w_e2 = win("w_e2", (NE, NC, FF, 128))
    d_be2 = win("b_e2", (NE, 128, 8))
    w_out = win("w_out", (NC, D, 128))
    d_esel = win("c_esel", (8, 8 * 128))
    d_decay = win("c_decay", (S, S))
    d_peT = win("c_peT", (D, S))
    d_ident = win("c_ident", (128, 192))
    d_tri = win("c_tri", (128, 128))

    def scratch(name, shape):
        return nc.dram_tensor(name, list(shape), F32, kind="Internal").ap()

    s_xn = scratch("s_xn", (D, S))
    s_x1 = scratch("s_x1", (S, D))
    s_y = scratch("s_y", (D, S))
    s_R = scratch("s_R", (D, S))
    s_h1 = scratch("s_h1", (D, S))
    s_h2 = scratch("s_h2", (D, S))
    s_h2p = scratch("s_h2p", (D, S))
    s_moe = scratch("s_moe", (D, S))

    def scratch_bf(name, shape):
        return nc.dram_tensor(name, list(shape), mybir.dt.bfloat16,
                              kind="Internal").ap()

    s_f1h = scratch_bf("s_f1h", (FF // 128, D, 128))
    s_f1l = scratch_bf("s_f1l", (FF // 128, D, 128))
    s_f2h = scratch_bf("s_f2h", (NC, FF, 128))
    s_f2l = scratch_bf("s_f2l", (NC, FF, 128))
    s_e1h = scratch_bf("s_e1h", (NE, FF // 128, D, 128))
    s_e1l = scratch_bf("s_e1l", (NE, FF // 128, D, 128))
    s_e2h = scratch_bf("s_e2h", (NE, NC, FF, 128))
    s_e2l = scratch_bf("s_e2l", (NE, NC, FF, 128))
    s_h1h = scratch_bf("s_h1h", (D, S))
    s_h1l = scratch_bf("s_h1l", (D, S))
    s_h2h = scratch_bf("s_h2h", (D, S))
    s_h2l = scratch_bf("s_h2l", (D, S))
    s_bd = scratch("s_bd", (2, 128 * S))

    d_tap = None
    if tap is not None:
        d_tap = nc.dram_tensor("tap", [S, D], F32, kind="ExternalOutput").ap()

    p = Prog()
    sbuf_t = nc.alloc_sbuf_tensor("arena", [128, SB_CAP], F32).ap()
    sba = SBAlloc(sbuf_t, SB_CAP, prog=p)

    r_ident = sba.alloc("ident", 192)
    r_tri = sba.alloc("tri", 128)
    r_ones = sba.alloc("ones", 128)
    r_stats = sba.alloc("stats", 16)
    r_acc = sba.alloc("acc", 16384)

    ps = [nc.alloc_psum_tensor(f"ps{i}", [128, 512], F32).ap() for i in range(8)]
    pb = [_Buf(f"ps{i}") for i in range(8)]

    dram_bufs = {}

    def dbuf(name):
        if name not in dram_bufs:
            dram_bufs[name] = _Buf(name)
        return dram_bufs[name]

    W_IN = dbuf("w_in")

    def load(dst_ap, src_ap, reads=(), writes=(), accs=(), eng="q"):
        return p.d(eng, lambda e, o=dst_ap, i=src_ap: e.dma_start(out=o, in_=i),
                   reads=reads, writes=writes, accs=accs)

    def store(dst_ap, src_ap, reads=(), writes=(), accs=(), eng="g"):
        return p.d(eng, lambda e, o=dst_ap, i=src_ap: e.dma_start(out=o, in_=i),
                   reads=reads, writes=writes, accs=accs)

    def mm(ps_ap, ps_buf, lhsT, rhs, reads, start, stop):
        fn = lambda e, o=ps_ap, a=lhsT, b=rhs, s0=start, s1=stop: e.matmul(
            o, a, b, start=s0, stop=s1)
        if start:
            return p.c("t", fn, reads=reads, writes=(ps_buf,))
        return p.c("t", fn, reads=reads, accs=(ps_buf,))

    def transpose(ps_ap, ps_buf, in_ap, reads, start=True, pdim=128, pbase=0):
        if pdim == 128:
            ident_ap = r_ident.ap(0, 128, 0, 128)
        elif pdim == 64:
            ident_ap = r_ident.ap(pbase, 64, 128, 64)
        else:
            assert pbase == 0 and pdim <= 128
            ident_ap = r_ident.ap(0, pdim, 0, pdim)
        fn = lambda e, o=ps_ap, i=in_ap, idp=ident_ap: e.transpose(o, i, idp)
        if start:
            return p.c("t", fn, reads=tuple(reads) + (r_ident.buf,), writes=(ps_buf,))
        return p.c("t", fn, reads=tuple(reads) + (r_ident.buf,), accs=(ps_buf,))

    def copy(eng, dst_ap, src_ap, reads, writes=(), accs=()):
        return p.c(eng, lambda e, o=dst_ap, i=src_ap: e.tensor_copy(out=o, in_=i),
                   reads=reads, writes=writes, accs=accs)

    def load_panel_bf(region, coff, dram_t, K, idx, buf, eng="q", k0=0, kn=None):
        if kn is None:
            kn = K
        kc = kn // 128
        src_ = dram_t[idx] if isinstance(idx, int) else dram_t[idx[0], idx[1]]
        a = src_[k0:k0 + kn, :].rearrange("(c pp) j -> pp c j", pp=128)
        dst = region.ap(0, 128, coff, kc * 64).bitcast(
            mybir.dt.bfloat16).rearrange("pp (c j) -> pp c j", j=128)
        return load(dst, a, reads=(W_IN,), writes=(buf,), eng=eng)

    def load_panel(region, coff, dram_t, K, idx, buf=None, eng="q", k0=0,
                   kn=None, accbuf=None):
        if kn is None:
            kn = K
        kc = kn // 128
        src = dram_t[idx] if isinstance(idx, int) else dram_t[idx[0], idx[1]]
        a = src[k0:k0 + kn, :].rearrange("(c pp) j -> pp c j", pp=128)
        dst = region.ap(0, 128, coff, kc * 128).rearrange("pp (c j) -> pp c j", j=128)
        if accbuf is not None:
            return load(dst, a, reads=(W_IN,), accs=(accbuf,), eng=eng)
        if buf is None:
            return load(dst, a, reads=(W_IN,), accs=(region.buf,), eng=eng)
        return load(dst, a, reads=(W_IN,), writes=(buf,), eng=eng)

    load(r_ident.ap(0, 128, 0, 192), d_ident, reads=(W_IN,), writes=(r_ident.buf,))
    load(r_tri.ap(0, 128, 0, 128), d_tri, reads=(W_IN,), writes=(r_tri.buf,))
    p.c("v", lambda e: e.memset(r_ones.ap(), 1.0), writes=(r_ones.buf,))

    r_cimm = sba.alloc("cimm", 8)
    C_NMLA, C_NXL, C_SXL, C_SMLA = (r_cimm.ap(0, 128, i, 1) for i in range(4))
    for i, val in enumerate((-C_MLA, -C_XL, SCALE_XL, SCALE_MLA)):
        p.c("v", lambda e, a=r_cimm.ap(0, 128, i, 1), v=float(val): e.memset(a, v),
            writes=(r_cimm.buf,) if i == 0 else (), accs=(r_cimm.buf,) if i > 0 else ())

    def ln_rowwise(x_ap, x_buf, out_ap, out_buf, scr_ap, scr_buf, width, gb=None):
        st = r_stats
        sbuf = st.buf
        ssum, ssq = st.ap(0, 128, 0, 1), st.ap(0, 128, 1, 1)
        mean, var = st.ap(0, 128, 2, 1), st.ap(0, 128, 3, 1)
        rstd, nmean = st.ap(0, 128, 4, 1), st.ap(0, 128, 5, 1)
        m2 = st.ap(0, 128, 6, 1)
        p.c("v", lambda e, i=x_ap, o=ssum: e.reduce_sum(o, i, axis=AX),
            reads=(x_buf,), accs=(sbuf,))
        p.c("s", lambda e, i=x_ap, o=scr_ap, a=ssq: e.activation(
            o, i, AF.Square, accum_out=a),
            reads=(x_buf,), writes=(scr_buf,), accs=(sbuf,))
        p.c("v", lambda e, o=mean, i=ssum, w=width: e.tensor_scalar(
            out=o, in0=i, scalar1=1.0 / w, scalar2=None, op0=ALU.mult), accs=(sbuf,))
        p.c("v", lambda e, o=m2, i=mean: e.tensor_tensor(out=o, in0=i, in1=i, op=ALU.mult),
            accs=(sbuf,))
        p.c("v", lambda e, o=var, i=ssq, w=width: e.tensor_scalar(
            out=o, in0=i, scalar1=1.0 / w, scalar2=None, op0=ALU.mult), accs=(sbuf,))
        p.c("v", lambda e, o=var, a=m2: e.tensor_tensor(out=o, in0=o, in1=a, op=ALU.subtract),
            accs=(sbuf,))
        p.c("v", lambda e, o=var: e.tensor_scalar(
            out=o, in0=o, scalar1=float(LN_EPS), scalar2=None, op0=ALU.add),
            accs=(sbuf,))
        p.c("s", lambda e, o=var: e.activation(o, o, AF.Sqrt), accs=(sbuf,))
        p.c("v", lambda e, o=rstd, i=var: e.reciprocal(o, i), accs=(sbuf,))
        p.c("v", lambda e, o=nmean, i=mean: e.tensor_scalar(
            out=o, in0=i, scalar1=-1.0, scalar2=None, op0=ALU.mult), accs=(sbuf,))
        p.c("v", lambda e, o=out_ap, i=x_ap, a=nmean, b=rstd: e.tensor_scalar(
            out=o, in0=i, scalar1=a, scalar2=b, op0=ALU.add, op1=ALU.mult),
            reads=(x_buf, sbuf), accs=(out_buf,))
        if gb is not None:
            g_ap, b_ap, g_buf = gb
            p.c("v", lambda e, o=out_ap, g=g_ap: e.tensor_tensor(
                out=o, in0=o, in1=g, op=ALU.mult), reads=(g_buf,), accs=(out_buf,))
            p.c("v", lambda e, o=out_ap, g=b_ap: e.tensor_tensor(
                out=o, in0=o, in1=g, op=ALU.add), reads=(g_buf,), accs=(out_buf,))

    def replicate_ln_params(idx_g, idx_b):
        r_gb = sba.alloc("lngb", 2048)
        tmp = sba.alloc("lntmp", 2048)
        tb = tmp.mkbuf("lntmp")
        load(tmp.ap(0, 1, 0, 1024), d_ln[idx_g:idx_g + 1, :], reads=(W_IN,), writes=(tb,))
        load(tmp.ap(0, 1, 1024, 1024), d_ln[idx_b:idx_b + 1, :], reads=(W_IN,), accs=(tb,))
        for k in range(4):
            mm(ps[7][:, 0:512], pb[7], r_ones.ap(0, 1, 0, 128),
               tmp.ap(0, 1, k * 512, 512), reads=(r_ones.buf, tb),
               start=True, stop=True)
            copy("v", r_gb.ap(0, 128, k * 512, 512), ps[7][:, 0:512],
                 reads=(pb[7],), accs=(r_gb.buf,))
        tmp.free()
        return r_gb

    def fm_to_tm_tile(t, get_block, ta, tbuf):
        for g in range(2):
            pi = 4 + g
            for j in range(4):
                kc = g * 4 + j
                blk, rds = get_block(kc)
                transpose(ps[pi][:, j * 128:(j + 1) * 128], pb[pi], blk,
                          reads=rds, start=(j == 0))
            copy("v", ta[:, g * 512:(g + 1) * 512], ps[pi][:, 0:512],
                 reads=(pb[pi],), accs=(tbuf,))

    def tm_to_fm_store(t, na, nbuf, rf, fb, sl, dst_dram, dst_dbuf,
                       bf_pair=None, rbf=None, bfb=None):
        for g in range(2):
            pi = 6 + g
            for j in range(4):
                kc = g * 4 + j
                transpose(ps[pi][:, j * 128:(j + 1) * 128], pb[pi],
                          na[:, kc * 128:(kc + 1) * 128], reads=(nbuf,),
                          start=(j == 0))
            fa = rf.ap(0, 128, sl * 1024 + g * 512, 512)
            copy("v", fa, ps[pi][:, 0:512], reads=(pb[pi],), writes=(fb[sl],))
            dstap = bass.AP(dst_dram.tensor,
                            dst_dram.offset + (g * 4 * 128) * S + t * 128,
                            [[S, 128], [128 * S, 4], [1, 128]])
            store(dstap, fa.rearrange("pp (c j) -> pp c j", j=128),
                  reads=(fb[sl],), accs=(dst_dbuf,))
            if bf_pair is not None:
                hi_d, lo_d = bf_pair
                ha = rbf.ap(0, 128, sl * 512 + g * 256, 256).bitcast(
                    mybir.dt.bfloat16)
                la = rbf.ap(0, 128, 1024 + sl * 512 + g * 256, 256).bitcast(
                    mybir.dt.bfloat16)
                p.c("s", lambda e, o=ha, i=fa: e.activation(o, i, AF.Copy),
                    reads=(fb[sl],), accs=(bfb[sl],))
                p.c("v", lambda e, o=la, a=fa, b=ha:
                    e.tensor_tensor(out=o, in0=a, in1=b, op=ALU.subtract),
                    reads=(fb[sl],), accs=(bfb[sl],))
                for dd, aa in ((hi_d, ha), (lo_d, la)):
                    dstap2 = bass.AP(dd.tensor,
                                     dd.offset + (g * 4 * 128) * S + t * 128,
                                     [[S, 128], [128 * S, 4], [1, 128]])
                    store(dstap2, aa.rearrange("pp (c j) -> pp c j", j=128),
                          reads=(bfb[sl],), accs=(dst_dbuf,))

    def make_split_shards():
        jobs = []
        for n in range(FF // 128):
            jobs.append((w_ff1, n, D, s_f1h, s_f1l, "ws_f1"))
        for m in range(NC):
            jobs.append((w_ff2, m, FF, s_f2h, s_f2l, "ws_f2"))
        for e in range(NE):
            for n in range(FF // 128):
                jobs.append((w_e1, (e, n), D, s_e1h, s_e1l, f"ws_e1_{e}"))
            for m in range(NC):
                jobs.append((w_e2, (e, m), FF, s_e2h, s_e2l, f"ws_e2_{e}"))
        r_ws = sba.alloc("wsplit", 2048)
        wsb = r_ws.mkbufs("wsplit", 1)

        def do_job(job, slot):
            srct, idx, K, hid, lod, dbn = job
            coff = slot * 2048
            for k0 in range(0, K, 1024):
                load_panel(r_ws, coff, srct, K, idx, buf=wsb[slot], eng="s",
                           k0=k0, kn=1024)
                p.c("s", lambda e, o=r_ws.ap(0, 128, coff + 1024, 512).bitcast(
                    mybir.dt.bfloat16), i=r_ws.ap(0, 128, coff, 1024):
                    e.activation(o, i, AF.Copy), reads=(wsb[slot],),
                    accs=(wsb[slot],))
                p.c("v", lambda e, o=r_ws.ap(0, 128, coff + 1536, 512).bitcast(
                    mybir.dt.bfloat16), a=r_ws.ap(0, 128, coff, 1024),
                    b=r_ws.ap(0, 128, coff + 1024, 512).bitcast(mybir.dt.bfloat16):
                    e.tensor_tensor(out=o, in0=a, in1=b, op=ALU.subtract),
                    accs=(wsb[slot],))
                dsth = hid[idx] if isinstance(idx, int) else hid[idx[0], idx[1]]
                dstl = lod[idx] if isinstance(idx, int) else lod[idx[0], idx[1]]
                store(dsth[k0:k0 + 1024, :].rearrange("(c pp) j -> pp c j", pp=128),
                      r_ws.ap(0, 128, coff + 1024, 512).bitcast(
                          mybir.dt.bfloat16).rearrange("pp (c j) -> pp c j", j=128),
                      reads=(wsb[slot],), accs=(dbuf(dbn),), eng="q")
                store(dstl[k0:k0 + 1024, :].rearrange("(c pp) j -> pp c j", pp=128),
                      r_ws.ap(0, 128, coff + 1536, 512).bitcast(
                          mybir.dt.bfloat16).rearrange("pp (c j) -> pp c j", j=128),
                      reads=(wsb[slot],), accs=(dbuf(dbn),), eng="q")

        state = {"i": 0, "slot": 0}

        def emit(njobs):
            while njobs > 0 and state["i"] < len(jobs):
                do_job(jobs[state["i"]], state["slot"])
                state["i"] += 1
                state["slot"] = 0
                njobs -= 1

        def finish():
            emit(len(jobs))
            r_ws.free()
        return emit, finish

    def stage_ln_x():
        rx = sba.alloc("s1x", 2048)
        rn = sba.alloc("s1n", 2048)
        rs = sba.alloc("s1s", 1024)
        rf = sba.alloc("s1f", 2048)
        xb = rx.mkbufs("s1x", 2)
        nb = rn.mkbufs("s1n", 2)
        fb = rf.mkbufs("s1f", 2)
        for t in range(NT):
            sl = t % 2
            xa = rx.ap(0, 128, sl * 1024, 1024)
            na = rn.ap(0, 128, sl * 1024, 1024)
            load(xa, dx[t * 128:(t + 1) * 128, :], reads=(W_IN,), writes=(xb[sl],))
            ln_rowwise(xa, xb[sl], na, nb[sl], rs.ap(), rs.buf, D)
            if tap == "xn":
                store(d_tap[t * 128:(t + 1) * 128, :], na, reads=(nb[sl],),
                      accs=(dbuf("tap"),))
            tm_to_fm_store(t, na, nb[sl], rf, fb, sl, s_xn, dbuf("xn"))
        rx.free(); rn.free(); rs.free(); rf.free()

    def stage_lat():
        r_lat = sba.alloc("lat", 2 * 2048)
        r_wkv = sba.alloc("wkv", 2 * 3 * 1024)
        wkvb = r_wkv.mkbufs("wkv", 2)
        r_wd = sba.alloc("wd", 2 * 1024)
        r_xs = sba.alloc("s2x", 8 * 512)
        r_kv = sba.alloc("s2kv", 2 * 512)
        xsb = r_xs.mkbufs("s2x", 8)
        kvb = r_kv.mkbufs("s2kv", 2)
        for lp in range(2):
            load_panel(r_wd, lp * 1024, w_dn, D, lp)
        for stripe in range(4):
            q0 = stripe * 512
            for n in range(NC):
                wko = (n % 2) * 3072
                wkb = wkvb[n % 2]
                load_panel(r_wkv, wko, w_k, D, n, buf=wkb)
                load_panel(r_wkv, wko + 1024, w_v, D, n, buf=None, accbuf=wkb)
                p.c("v", lambda e, o=r_wkv.ap(0, 128, wko + 2048, 1024),
                    a=r_wkv.ap(0, 128, wko, 1024), b=r_wkv.ap(0, 128, wko + 1024, 1024):
                    e.tensor_tensor(out=o, in0=a, in1=b, op=ALU.add),
                    accs=(wkb,))
                kvp = n % 2
                for kc in range(NC):
                    sl = kc
                    xa = r_xs.ap(0, 128, sl * 512, 512)
                    load(xa, s_xn[kc * 128:(kc + 1) * 128, q0:q0 + 512],
                         reads=(dbuf("xn"),), writes=(xsb[sl],))
                    mm(ps[0 if kvp == 0 else 3][:, 0:512], pb[0 if kvp == 0 else 3],
                       r_wkv.ap(0, 128, wko + 2048 + kc * 128, 128),
                       xa, reads=(wkb, xsb[sl]), start=(kc == 0),
                       stop=(kc == NC - 1))
                copy("v", r_kv.ap(0, 128, kvp * 512, 512),
                     ps[0 if kvp == 0 else 3][:, 0:512],
                     reads=(pb[0 if kvp == 0 else 3],), writes=(kvb[kvp],))
                for lp in range(2):
                    mm(ps[1 + lp][:, 0:512], pb[1 + lp],
                       r_wd.ap(0, 128, lp * 1024 + n * 128, 128),
                       r_kv.ap(0, 128, kvp * 512, 512),
                       reads=(r_wd.buf, kvb[kvp]), start=(n == 0), stop=(n == NC - 1))
            for lp in range(2):
                copy("v", r_lat.ap(0, 128, lp * 2048 + q0, 512), ps[1 + lp][:, 0:512],
                     reads=(pb[1 + lp],), accs=(r_lat.buf,))
        r_wkv.free(); r_wd.free(); r_xs.free(); r_kv.free()
        return r_lat

    def wo_accumulate(hp, r_of, ofb, w_dram, first_pair):
        r_wt = sba.alloc("wot", 2 * 128)
        wtb = r_wt.mkbufs("wo", 2)
        for n in range(NC):
            sl = n % 2
            wa = r_wt.ap(0, 128, sl * 128, 128)
            load(wa, w_dram[n][hp * 128:(hp + 1) * 128, :], reads=(W_IN,),
                 writes=(wtb[sl],))
            for stripe in range(4):
                pi = 4 + (stripe % 2)
                mm(ps[pi][:, 0:512], pb[pi], wa, r_of.ap(0, 128, stripe * 512, 512),
                   reads=(wtb[sl], ofb), start=True, stop=True)
                dst = r_acc.ap(0, 128, n * 2048 + stripe * 512, 512)
                if first_pair:
                    copy("v", dst, ps[pi][:, 0:512], reads=(pb[pi],), accs=(r_acc.buf,))
                else:
                    p.c("v", lambda e, o=dst, i=ps[pi][:, 0:512]:
                        e.tensor_tensor(out=o, in0=o, in1=i, op=ALU.add),
                        reads=(pb[pi],), accs=(r_acc.buf,))
        r_wt.free()

    def stage_mla(r_lat, split_emit=lambda n: None):
        r_q = sba.alloc("mq", 2048)
        r_kr = sba.alloc("mkr", 2048)
        r_vr = sba.alloc("mvr", 2048)
        r_vs = sba.alloc("mvs", 2 * 16 * 65)
        r_pt = sba.alloc("mpt", 4 * 512)
        r_dk = sba.alloc("mdk", 2 * 512)
        r_wp = sba.alloc("mwp", 2 * 1024)
        r_xs = sba.alloc("mxs", 8 * 512)
        r_oh = sba.alloc("moh", 2 * 512)
        r_rz = sba.alloc("mrz", 2 * 512)
        r_of = sba.alloc("mof", 2048)
        qb, krb, vrb = r_q.mkbuf("mq"), r_kr.mkbuf("mkr"), r_vr.mkbuf("mvr")
        vsb = r_vs.mkbufs("mvs", 2)
        ptb = r_pt.mkbufs("mpt", 4)
        dkb = r_dk.mkbufs("mdk", 2)
        xsb = r_xs.mkbufs("mxs", 8)
        ohb = r_oh.mkbufs("moh", 2)
        rzb = r_rz.mkbufs("mrz", 2)
        ofb = r_of.mkbuf("mof")

        for hp in range(NPAIR):
            split_emit(4)
            load_panel(r_wp, 0, w_q, D, hp)
            for stripe in range(4):
                q0 = stripe * 512
                qpi = stripe % 2
                for kc in range(NC):
                    sl = kc
                    xa = r_xs.ap(0, 128, sl * 512, 512)
                    load(xa, s_xn[kc * 128:(kc + 1) * 128, q0:q0 + 512],
                         reads=(dbuf("xn"),), writes=(xsb[sl],))
                    mm(ps[qpi][:, 0:512], pb[qpi], r_wp.ap(0, 128, kc * 128, 128), xa,
                       reads=(r_wp.buf, xsb[sl]), start=(kc == 0), stop=(kc == NC - 1))
                p.c("s", lambda e, o=r_q.ap(0, 128, q0, 512), i=ps[qpi][:, 0:512],
                    sc=C_SMLA: e.activation(o, i, AF.Copy, scale=sc),
                    reads=(pb[qpi], r_cimm.buf), accs=(qb,))
            load_panel(r_wp, 0, w_uk, DL, hp)
            load_panel(r_wp, 1024, w_uv, DL, hp)
            for stripe in range(4):
                q0 = stripe * 512
                for lp in range(2):
                    mm(ps[1][:, 0:512], pb[1], r_wp.ap(0, 128, lp * 128, 128),
                       r_lat.ap(0, 128, lp * 2048 + q0, 512),
                       reads=(r_wp.buf, r_lat.buf), start=(lp == 0), stop=(lp == 1))
                    mm(ps[2][:, 0:512], pb[2], r_wp.ap(0, 128, 1024 + lp * 128, 128),
                       r_lat.ap(0, 128, lp * 2048 + q0, 512),
                       reads=(r_wp.buf, r_lat.buf), start=(lp == 0), stop=(lp == 1))
                copy("v", r_kr.ap(0, 128, q0, 512), ps[1][:, 0:512],
                     reads=(pb[1],), accs=(krb,))
                copy("v", r_vr.ap(0, 128, q0, 512), ps[2][:, 0:512],
                     reads=(pb[2],), accs=(vrb,))
            for h in range(2):
                ones_cols = r_vs.ap(0, 128, h * 1040, 1040).rearrange(
                    "pp (kt c) -> pp kt c", c=65)[:, :, 64:65]
                p.c("v", lambda e, a=ones_cols: e.memset(a, 1.0), accs=(vsb[h],))
                for kt in range(NT):
                    pi = 3 + (kt % 2)
                    transpose(ps[pi][:, 0:64], pb[pi],
                              r_vr.ap(h * 64, 64, kt * 128, 128), reads=(vrb,),
                              pdim=64, pbase=h * 64)
                    copy("v", r_vs.ap(0, 128, h * 1040 + kt * 65, 64), ps[pi][:, 0:64],
                         reads=(pb[pi],), accs=(vsb[h],))
            for qq in range(4):
                split_emit(2)
                q0 = qq * 512

                def emit_st(kt):
                    dsl = kt % 2
                    dk = r_dk.ap(0, 128, dsl * 512, 512)
                    load(dk, d_decay[kt * 128:(kt + 1) * 128, q0:q0 + 512],
                         reads=(W_IN,), writes=(dkb[dsl],))
                    for h in range(2):
                        spi = h * 2 + (kt % 2)
                        mm(ps[spi][:, 0:512], pb[spi],
                           r_kr.ap(h * 64, 64, kt * 128, 128),
                           r_q.ap(h * 64, 64, q0, 512),
                           reads=(krb, qb), start=True, stop=True)

                def emit_consume(kt):
                    dsl = kt % 2
                    dk = r_dk.ap(0, 128, dsl * 512, 512)
                    for h in range(2):
                        spi = h * 2 + (kt % 2)
                        pt = r_pt.ap(0, 128, spi * 512, 512)
                        pbuf = ptb[spi]
                        p.c("v", lambda e, o=pt, i=ps[spi][:, 0:512]:
                            e.tensor_scalar(out=o, in0=i, scalar1=CLAMP,
                                            scalar2=-CLAMP, op0=ALU.min, op1=ALU.max),
                            reads=(pb[spi],), writes=(pbuf,))
                        p.c("v", lambda e, o=pt, i=dk:
                            e.tensor_tensor(out=o, in0=o, in1=i, op=ALU.mult),
                            reads=(dkb[dsl],), accs=(pbuf,))
                        p.c("s", lambda e, o=pt, b=C_NMLA:
                            e.activation(o, o, AF.Exp, bias=b),
                            reads=(r_cimm.buf,), accs=(pbuf,))
                        mm(ps[6 + h][0:65, 0:512], pb[6 + h],
                           r_vs.ap(0, 128, h * 1040 + kt * 65, 65), pt,
                           reads=(vsb[h], pbuf), start=(kt == 0),
                           stop=(kt == NT - 1))

                emit_st(0)
                for kt in range(1, NT):
                    emit_st(kt)
                    emit_consume(kt - 1)
                emit_consume(NT - 1)
                for h in range(2):
                    oh = r_oh.ap(0, 65, h * 512, 512)
                    copy("v", oh, ps[6 + h][0:65, 0:512], reads=(pb[6 + h],),
                         writes=(ohb[h],))
                    rz1 = r_rz.ap(64, 1, h * 512, 512)
                    p.c("v", lambda e, o=rz1, i=r_oh.ap(64, 1, h * 512, 512):
                        e.reciprocal(o, i), reads=(ohb[h],), writes=(rzb[h],))
                    mm(ps[h][0:64, 0:512], pb[h], r_ones.ap(64, 1, 0, 64), rz1,
                       reads=(r_ones.buf, rzb[h]), start=True, stop=True)
                    p.c("v", lambda e, o=r_of.ap(h * 64, 64, q0, 512),
                        a=r_oh.ap(0, 64, h * 512, 512), b=ps[h][0:64, 0:512]:
                        e.tensor_tensor(out=o, in0=a, in1=b, op=ALU.mult),
                        reads=(ohb[h], pb[h]), accs=(ofb,))
            wo_accumulate(hp, r_of, ofb, w_o, first_pair=(hp == 0))
        for r in (r_q, r_kr, r_vr, r_vs, r_pt, r_dk, r_wp, r_xs, r_oh, r_rz, r_of):
            r.free()

    def stage_x1_y():
        rx = sba.alloc("s4x", 2048)
        r1 = sba.alloc("s4o", 2048)
        rn = sba.alloc("s4n", 2048)
        rs = sba.alloc("s4s", 1024)
        rf = sba.alloc("s4f", 2048)
        xb = rx.mkbufs("s4x", 2)
        ob = r1.mkbufs("s4o", 2)
        nb = rn.mkbufs("s4n", 2)
        fb = rf.mkbufs("s4f", 2)
        for t in range(NT):
            sl = t % 2
            xa = rx.ap(0, 128, sl * 1024, 1024)
            x1 = r1.ap(0, 128, sl * 1024, 1024)
            na = rn.ap(0, 128, sl * 1024, 1024)
            load(xa, dx[t * 128:(t + 1) * 128, :], reads=(W_IN,), writes=(xb[sl],))
            for g in range(2):
                pi = 4 + g
                for j in range(4):
                    kc = g * 4 + j
                    transpose(ps[pi][:, j * 128:(j + 1) * 128], pb[pi],
                              r_acc.ap(0, 128, kc * 2048 + t * 128, 128),
                              reads=(r_acc.buf,), start=(j == 0))
                p.c("v", lambda e, o=x1[:, g * 512:(g + 1) * 512],
                    a=xa[:, g * 512:(g + 1) * 512], b=ps[pi][:, 0:512]:
                    e.tensor_tensor(out=o, in0=a, in1=b, op=ALU.add),
                    reads=(xb[sl], pb[pi]), accs=(ob[sl],))
            store(s_x1[t * 128:(t + 1) * 128, :], x1, reads=(ob[sl],),
                  accs=(dbuf("x1"),))
            if tap == "x1":
                store(d_tap[t * 128:(t + 1) * 128, :], x1, reads=(ob[sl],),
                      accs=(dbuf("tap"),))
            ln_rowwise(x1, ob[sl], na, nb[sl], rs.ap(), rs.buf, D)
            if tap == "y":
                store(d_tap[t * 128:(t + 1) * 128, :], na, reads=(nb[sl],),
                      accs=(dbuf("tap"),))
            tm_to_fm_store(t, na, nb[sl], rf, fb, sl, s_y, dbuf("y"))
        rx.free(); r1.free(); rn.free(); rs.free(); rf.free()

    def stage_R():
        r_wp = sba.alloc("rwp", 1024)
        r_pe = sba.alloc("rpe", 2 * 512)
        r_ev = sba.alloc("rev", 2 * 512)
        wpb = r_wp.mkbuf("rwp")
        peb = r_pe.mkbufs("rpe", 2)
        evb = r_ev.mkbufs("rev", 2)
        for n in range(NC):
            load_panel(r_wp, 0, w_r, D, n, buf=wpb)
            for stripe in range(4):
                q0 = stripe * 512
                for kc in range(NC):
                    sl = kc % 2
                    pa = r_pe.ap(0, 128, sl * 512, 512)
                    load(pa, d_peT[kc * 128:(kc + 1) * 128, q0:q0 + 512],
                         reads=(W_IN,), writes=(peb[sl],))
                    mm(ps[0][:, 0:512], pb[0], r_wp.ap(0, 128, kc * 128, 128), pa,
                       reads=(wpb, peb[sl]), start=(kc == 0), stop=(kc == NC - 1))
                esl = stripe % 2
                ea = r_ev.ap(0, 128, esl * 512, 512)
                copy("v", ea, ps[0][:, 0:512], reads=(pb[0],), writes=(evb[esl],))
                store(s_R[n * 128:(n + 1) * 128, q0:q0 + 512], ea,
                      reads=(evb[esl],), accs=(dbuf("R"),))
        r_wp.free(); r_pe.free(); r_ev.free()

    def stage_xl(split_emit=lambda n: None):
        r_qu = sba.alloc("xqu", 2048)
        r_qv = sba.alloc("xqv", 2048)
        r_k = sba.alloc("xk", 2048)
        r_r = sba.alloc("xr", 2048)
        r_v = sba.alloc("xv", 512)
        r_vs = sba.alloc("xvs", 2 * 16 * 65)
        r_P = sba.alloc("xP", 2048)
        r_PT = sba.alloc("xPT", 4 * 128)
        r_braw = sba.alloc("xbraw", 512)
        r_bskew = sba.alloc("xbskew", 2 * 2048)
        r_ys = sba.alloc("xys", 6 * 512)
        r_wp = sba.alloc("xwp", 3 * 1024)
        r_ub = sba.alloc("xub", 16)
        r_oh = sba.alloc("xoh", 2 * 128)
        r_rz = sba.alloc("xrz", 2 * 128)
        r_of = sba.alloc("xof", 2048)
        qub, qvb, kb = r_qu.mkbuf("xqu"), r_qv.mkbuf("xqv"), r_k.mkbuf("xk")
        rb, vb = r_r.mkbuf("xr"), r_v.mkbuf("xv")
        vsb = r_vs.mkbufs("xvs", 2)
        Pb = r_P.mkbuf("xP")
        PTb = r_PT.mkbufs("xPT", 4)
        brb = r_braw.mkbufs("xbr", 1)
        bsb = r_bskew.mkbufs("xbs", 2)
        ysb = r_ys.mkbufs("xys", 6)
        wpb = r_wp.mkbuf("xwp")
        ubb = r_ub.mkbuf("xub")
        ohb = r_oh.mkbufs("xoh", 2)
        rzb = r_rz.mkbufs("xrz", 2)
        ofb = r_of.mkbuf("xof")
        sbd_b = [dbuf("sbd0"), dbuf("sbd1")]

        load(r_ub.ap(0, 128, 0, 8), d_ub, reads=(W_IN,), writes=(ubb,))
        load(r_ub.ap(0, 128, 8, 8), d_vb, reads=(W_IN,), accs=(ubb,))
        for hp in range(NPAIR):
            split_emit(2)
            load_panel(r_wp, 0, w_qkv, D, hp, buf=wpb)
            load_panel(r_wp, 1024, w_qkv, D, 8 + hp)
            load_panel(r_wp, 2048, w_qkv, D, 16 + hp)
            load(r_r.ap(0, 128, 0, 2048), s_R[hp * 128:(hp + 1) * 128, :],
                 reads=(dbuf("R"),), writes=(rb,))
            for stripe in range(4):
                q0 = stripe * 512
                for kc in range(NC):
                    sl = kc % 6
                    ya = r_ys.ap(0, 128, sl * 512, 512)
                    load(ya, s_y[kc * 128:(kc + 1) * 128, q0:q0 + 512],
                         reads=(dbuf("y"),), writes=(ysb[sl],))
                    for ti in range(3):
                        mm(ps[ti][:, 0:512], pb[ti],
                           r_wp.ap(0, 128, ti * 1024 + kc * 128, 128), ya,
                           reads=(wpb if ti == 0 else r_wp.buf, ysb[sl]),
                           start=(kc == 0), stop=(kc == NC - 1))
                p.c("v", lambda e, o=r_qu.ap(0, 128, q0, 512), i=ps[0][:, 0:512],
                    u=r_ub.ap(0, 128, hp, 1): e.tensor_scalar(
                        out=o, in0=i, scalar1=u, scalar2=None, op0=ALU.add),
                    reads=(pb[0], ubb), accs=(qub,))
                p.c("v", lambda e, o=r_qv.ap(0, 128, q0, 512), i=ps[0][:, 0:512],
                    u=r_ub.ap(0, 128, 8 + hp, 1): e.tensor_scalar(
                        out=o, in0=i, scalar1=u, scalar2=None, op0=ALU.add),
                    reads=(pb[0], ubb), accs=(qvb,))
                copy("v", r_k.ap(0, 128, q0, 512), ps[1][:, 0:512],
                     reads=(pb[1],), accs=(kb,))
                copy("v", r_v.ap(0, 128, 0, 512), ps[2][:, 0:512],
                     reads=(pb[2],), writes=(vb,))
                for j in range(4):
                    kt = stripe * 4 + j
                    for h in range(2):
                        pi2 = 3 + ((j * 2 + h) % 2)
                        transpose(ps[pi2][:, 0:64], pb[pi2],
                                  r_v.ap(h * 64, 64, j * 128, 128), reads=(vb,),
                                  pdim=64, pbase=h * 64)
                        copy("v", r_vs.ap(0, 128, h * 1040 + kt * 65, 64),
                             ps[pi2][:, 0:64], reads=(pb[pi2],), accs=(vsb[h],))
            for h in range(2):
                ones_cols = r_vs.ap(0, 128, h * 1040, 1040).rearrange(
                    "pp (kt c) -> pp kt c", c=65)[:, :, 64:65]
                p.c("v", lambda e, a=ones_cols: e.memset(a, 1.0), accs=(vsb[h],))

            steps = [(h, qt) for h in range(2) for qt in range(NT)]

            def bd_chain2(step_i):
                h, qt = steps[step_i]
                slot = step_i % 2
                W = (qt + 1) * 128
                nch = (W + 511) // 512
                first = True
                for c in range(nch):
                    wc = min(512, W - c * 512)
                    pi2 = (c % 2)
                    mm(ps[pi2][:, 0:wc], pb[pi2],
                       r_qv.ap(h * 64, 64, qt * 128, 128),
                       r_r.ap(h * 64, 64, S - W + c * 512, wc),
                       reads=(qvb, rb), start=True, stop=True)
                    copy("v", r_braw.ap(0, 128, 0, wc),
                         ps[pi2][:, 0:wc], reads=(pb[pi2],), writes=(brb[0],))
                    dst = bass.AP(s_bd.tensor, s_bd.offset + slot * (128 * S) + c * 512,
                                  [[W, 128], [1, wc]])
                    store(dst, r_braw.ap(0, 128, 0, wc),
                          reads=(brb[0],),
                          writes=(sbd_b[slot],) if first else (),
                          accs=(sbd_b[slot],) if not first else (), eng="q")
                    first = False
                skew = bass.AP(s_bd.tensor, s_bd.offset + slot * (128 * S) + 127,
                               [[W - 1, 128], [1, W]])
                load(r_bskew.ap(0, 128, slot * 2048, W), skew,
                     reads=(sbd_b[slot],), writes=(bsb[slot],), eng="g")

            bd_chain2(0)
            for i, (h, qt) in enumerate(steps):
                split_emit(1)
                slot = i % 2
                W = (qt + 1) * 128
                nch = (W + 511) // 512
                for c in range(nch):
                    wc = min(512, W - c * 512)
                    pi2 = 2 + c
                    mm(ps[pi2][:, 0:wc], pb[pi2],
                       r_qu.ap(h * 64, 64, qt * 128, 128),
                       r_k.ap(h * 64, 64, c * 512, wc),
                       reads=(qub, kb), start=True, stop=True)
                if i + 1 < len(steps):
                    bd_chain2(i + 1)
                for c in range(nch):
                    wc = min(512, W - c * 512)
                    pi2 = 2 + c
                    p.c("v", lambda e, o=r_P.ap(0, 128, c * 512, wc),
                        a=ps[pi2][:, 0:wc],
                        b=r_bskew.ap(0, 128, slot * 2048 + c * 512, wc):
                        e.tensor_tensor(out=o, in0=a, in1=b, op=ALU.add),
                        reads=(pb[pi2], bsb[slot]),
                        writes=(Pb,) if c == 0 else (), accs=(Pb,) if c > 0 else ())
                    p.c("s", lambda e, o=r_P.ap(0, 128, c * 512, wc),
                        sc=C_SXL, b=C_NXL:
                        e.activation(o, o, AF.Exp, scale=sc, bias=b),
                        reads=(r_cimm.buf,), accs=(Pb,))
                p.c("v", lambda e, o=r_P.ap(0, 128, qt * 128, 128),
                    m=r_tri.ap(0, 128, 0, 128):
                    e.tensor_tensor(out=o, in0=o, in1=m, op=ALU.mult),
                    reads=(r_tri.buf,), accs=(Pb,))

                def emit_tr(kt):
                    pi2 = 4 + (kt % 2)
                    ptsl = kt % 4
                    transpose(ps[pi2][:, 0:128], pb[pi2],
                              r_P.ap(0, 128, kt * 128, 128), reads=(Pb,))
                    copy("v", r_PT.ap(0, 128, ptsl * 128, 128), ps[pi2][:, 0:128],
                         reads=(pb[pi2],), writes=(PTb[ptsl],))

                def emit_av(kt):
                    ptsl = kt % 4
                    mm(ps[6][0:65, 0:128], pb[6],
                       r_vs.ap(0, 128, h * 1040 + kt * 65, 65),
                       r_PT.ap(0, 128, ptsl * 128, 128),
                       reads=(vsb[h], PTb[ptsl]), start=(kt == 0), stop=(kt == qt))

                emit_tr(0)
                for kt in range(1, qt + 1):
                    emit_tr(kt)
                    emit_av(kt - 1)
                emit_av(qt)
                oh = r_oh.ap(0, 65, h * 128, 128)
                copy("v", oh, ps[6][0:65, 0:128], reads=(pb[6],), writes=(ohb[h],))

                rz1 = r_rz.ap(64, 1, h * 128, 128)
                p.c("v", lambda e, o=rz1, i=r_oh.ap(64, 1, h * 128, 128):
                    e.reciprocal(o, i), reads=(ohb[h],), writes=(rzb[h],))
                mm(ps[7][0:64, 0:128], pb[7], r_ones.ap(64, 1, 0, 64), rz1,
                   reads=(r_ones.buf, rzb[h]), start=True, stop=True)
                p.c("v", lambda e, o=r_of.ap(h * 64, 64, qt * 128, 128),
                    a=r_oh.ap(0, 64, h * 128, 128), b=ps[7][0:64, 0:128]:
                    e.tensor_tensor(out=o, in0=a, in1=b, op=ALU.mult),
                    reads=(ohb[h], pb[7]), accs=(ofb,))

            wo_accumulate(hp, r_of, ofb, w_fo, first_pair=(hp == 0))
        for r in (r_qu, r_qv, r_k, r_r, r_v, r_vs, r_P, r_PT, r_braw, r_bskew,
                  r_ys, r_wp, r_ub, r_oh, r_rz, r_of):
            r.free()

    def stage_h1():
        r_gb = replicate_ln_params(0, 1)
        ry = sba.alloc("s7y", 1024)
        rt = sba.alloc("s7t", 2048)
        rn = sba.alloc("s7n", 2048)
        rs = sba.alloc("s7s", 1024)
        rf = sba.alloc("s7f", 2048)
        rbf = sba.alloc("s7bf", 2048)
        yb = ry.mkbufs("s7y", 4)
        tb = rt.mkbufs("s7t", 2)
        nb = rn.mkbufs("s7n", 2)
        fb = rf.mkbufs("s7f", 2)
        bfb = rbf.mkbufs("s7bf", 2)
        for t in range(NT):
            sl = t % 2
            ta = rt.ap(0, 128, sl * 1024, 1024)
            na = rn.ap(0, 128, sl * 1024, 1024)

            def get_block(kc, t=t):
                ysl = kc % 4
                ya = ry.ap(0, 128, ysl * 128, 128)
                load(ya, s_y[kc * 128:(kc + 1) * 128, t * 128:(t + 1) * 128],
                     reads=(dbuf("y"),), writes=(yb[ysl],))
                p.c("v", lambda e, o=ya, a=r_acc.ap(0, 128, kc * 2048 + t * 128, 128):
                    e.tensor_tensor(out=o, in0=o, in1=a, op=ALU.add),
                    reads=(r_acc.buf,), accs=(yb[ysl],))
                return ya, (yb[ysl],)

            fm_to_tm_tile(t, get_block, ta, tb[sl])
            if tap == "h1p":
                store(d_tap[t * 128:(t + 1) * 128, :], ta, reads=(tb[sl],),
                      accs=(dbuf("tap"),))
            ln_rowwise(ta, tb[sl], na, nb[sl], rs.ap(), rs.buf, D,
                       gb=(r_gb.ap(0, 128, 0, 1024), r_gb.ap(0, 128, 1024, 1024),
                           r_gb.buf))
            if tap == "h1":
                store(d_tap[t * 128:(t + 1) * 128, :], na, reads=(nb[sl],),
                      accs=(dbuf("tap"),))
            tm_to_fm_store(t, na, nb[sl], rf, fb, sl, s_h1, dbuf("h1"),
                           bf_pair=(s_h1h, s_h1l), rbf=rbf, bfb=bfb)
        ry.free(); rt.free(); rn.free(); rs.free(); rf.free(); rbf.free()
        r_gb.free()

    def split_bf16(hi_ap, lo_ap, src_ap, src_buf, hi_buf, lo_buf, eng1="s",
                   eng2="v"):
        if eng1 == "s":
            p.c("s", lambda e, o=hi_ap, i=src_ap: e.activation(o, i, AF.Copy),
                reads=(src_buf,), accs=(hi_buf,))
        else:
            p.c(eng1, lambda e, o=hi_ap, i=src_ap: e.tensor_copy(out=o, in_=i),
                reads=(src_buf,), accs=(hi_buf,))
        p.c(eng2, lambda e, o=lo_ap, a=src_ap, b=hi_ap:
            e.tensor_tensor(out=o, in0=a, in1=b, op=ALU.subtract),
            reads=(src_buf, hi_buf), accs=(lo_buf,))

    def ffn_expert(h_dram, h_hi_dram, h_lo_dram, h_dbuf, w1h, w1l, w2h, w2l,
                   wsrc_dbuf, get_i1, get_i2, be1_src, be2_src, out_hook, name,
                   need_h32):
        r_h = sba.alloc(f"{name}h", 8 * 512) if need_h32 else None
        r_hx = sba.alloc(f"{name}hx", 8 * 512)
        r_hid = sba.alloc(f"{name}hid", 32 * 512)
        r_hs = sba.alloc(f"{name}hs", 2 * 512)
        r_w1 = sba.alloc(f"{name}w1", 3 * 1024)
        r_w2 = sba.alloc(f"{name}w2", 2 * 2048)
        r_b = sba.alloc(f"{name}b", 40)
        hb = r_h.mkbuf("fh") if need_h32 else None
        hxb = r_hx.mkbuf("fhx")
        hidb = r_hid.mkbufs("fhid", 32)
        hsb = r_hs.mkbufs("fhs", 2)
        w1b = r_w1.mkbufs("fw1", 3)
        w2b = r_w2.mkbufs("fw2", 2)
        bb = r_b.mkbuf("fb")

        def hx_hi(kc, w=512):
            return r_hx.ap(0, 128, kc * 256, w // 2).bitcast(mybir.dt.bfloat16)

        def hx_lo(kc, w=512):
            return r_hx.ap(0, 128, 2048 + kc * 256, w // 2).bitcast(mybir.dt.bfloat16)

        def hid_hi(n):
            return r_hid.ap(0, 128, n * 256, 256).bitcast(mybir.dt.bfloat16)

        def hid_lo(n):
            return r_hid.ap(0, 128, 8192 + n * 256, 256).bitcast(mybir.dt.bfloat16)

        load(r_b.ap(0, 128, 0, 32), be1_src, reads=(W_IN,), writes=(bb,), eng="g")
        load(r_b.ap(0, 128, 32, 8), be2_src, reads=(W_IN,), accs=(bb,), eng="g")
        for stripe in range(4):
            q0 = stripe * 512
            for kc in range(NC):
                if need_h32:
                    load(r_h.ap(0, 128, kc * 512, 512),
                         h_dram[kc * 128:(kc + 1) * 128, q0:q0 + 512],
                         reads=(h_dbuf,), writes=(hb,) if kc == 0 else (),
                         accs=(hb,) if kc > 0 else (), eng="s")
                load(hx_hi(kc), h_hi_dram[kc * 128:(kc + 1) * 128, q0:q0 + 512],
                     reads=(h_dbuf,), writes=(hxb,) if kc == 0 else (),
                     accs=(hxb,) if kc > 0 else (), eng="s")
                load(hx_lo(kc), h_lo_dram[kc * 128:(kc + 1) * 128, q0:q0 + 512],
                     reads=(h_dbuf,), accs=(hxb,), eng="s")
            for n in range(FF // 128):
                wsl = n % 3
                i1 = get_i1(n)
                load_panel_bf(r_w1, wsl * 1024, w1h, D, i1, buf=w1b[wsl])
                load_panel_bf(r_w1, wsl * 1024 + 512, w1l, D, i1, buf=w1b[wsl])
                pi = n % 2
                first = True
                for kc in range(NC):
                    whi = r_w1.ap(0, 128, wsl * 1024 + kc * 64, 64).bitcast(
                        mybir.dt.bfloat16)
                    wlo = r_w1.ap(0, 128, wsl * 1024 + 512 + kc * 64, 64).bitcast(
                        mybir.dt.bfloat16)
                    mm(ps[pi][:, 0:512], pb[pi], whi, hx_hi(kc),
                       reads=(w1b[wsl], hxb), start=first, stop=False)
                    first = False
                    mm(ps[pi][:, 0:512], pb[pi], whi, hx_lo(kc),
                       reads=(w1b[wsl], hxb), start=False, stop=False)
                    mm(ps[pi][:, 0:512], pb[pi], wlo, hx_hi(kc),
                       reads=(w1b[wsl], hxb), start=False, stop=(kc == NC - 1))
                ssl = n % 2
                scr = r_hs.ap(0, 128, ssl * 512, 512)
                p.c("s", lambda e, o=scr, i=ps[pi][:, 0:512],
                    b1=r_b.ap(0, 128, n, 1): e.activation(o, i, AF.Relu, bias=b1),
                    reads=(pb[pi], bb), writes=(hsb[ssl],))
                p.c("v", lambda e, o=hid_hi(n), i=scr: e.tensor_copy(out=o, in_=i),
                    reads=(hsb[ssl],), writes=(hidb[n],))
                p.c("v", lambda e, o=hid_lo(n), a=scr, b=hid_hi(n):
                    e.tensor_tensor(out=o, in0=a, in1=b, op=ALU.subtract),
                    reads=(hsb[ssl],), accs=(hidb[n],))
            for m in range(NC):
                pi = 2 + (m % 2)
                first = True
                for half in range(2):
                    wsl = half
                    i2 = get_i2(m)
                    load_panel_bf(r_w2, wsl * 2048, w2h, FF, i2, buf=w2b[wsl],
                                  k0=half * 2048, kn=2048, eng="g")
                    load_panel_bf(r_w2, wsl * 2048 + 1024, w2l, FF, i2,
                                  buf=w2b[wsl], k0=half * 2048, kn=2048, eng="g")
                    for j in range(16):
                        n = half * 16 + j
                        whi = r_w2.ap(0, 128, wsl * 2048 + j * 64, 64).bitcast(
                            mybir.dt.bfloat16)
                        wlo = r_w2.ap(0, 128, wsl * 2048 + 1024 + j * 64, 64).bitcast(
                            mybir.dt.bfloat16)
                        mm(ps[pi][:, 0:512], pb[pi], whi, hid_hi(n),
                           reads=(w2b[wsl], hidb[n]), start=first, stop=False)
                        first = False
                        mm(ps[pi][:, 0:512], pb[pi], whi, hid_lo(n),
                           reads=(w2b[wsl], hidb[n]), start=False, stop=False)
                        mm(ps[pi][:, 0:512], pb[pi], wlo, hid_hi(n),
                           reads=(w2b[wsl], hidb[n]), start=False,
                           stop=(n == FF // 128 - 1))
                out_hook(m, stripe, ps[pi][:, 0:512], pb[pi], r_b, bb, r_h, hb)
        if need_h32:
            r_h.free()
        r_hx.free(); r_hid.free(); r_hs.free()
        r_w1.free(); r_w2.free(); r_b.free()

    def stage_h2():
        r_gb = replicate_ln_params(2, 3)
        rh2 = sba.alloc("s9h", 1024)
        rt = sba.alloc("s9t", 2048)
        rn = sba.alloc("s9n", 2048)
        rs = sba.alloc("s9s", 1024)
        rf = sba.alloc("s9f", 2048)
        rbf = sba.alloc("s9bf", 2048)
        h2b = rh2.mkbufs("s9h", 4)
        tb = rt.mkbufs("s9t", 2)
        nb = rn.mkbufs("s9n", 2)
        fb = rf.mkbufs("s9f", 2)
        bfb = rbf.mkbufs("s9bf", 2)
        for t in range(NT):
            sl = t % 2
            ta = rt.ap(0, 128, sl * 1024, 1024)
            na = rn.ap(0, 128, sl * 1024, 1024)

            def get_block(kc, t=t):
                ysl = kc % 4
                ha = rh2.ap(0, 128, ysl * 128, 128)
                load(ha, s_h2p[kc * 128:(kc + 1) * 128, t * 128:(t + 1) * 128],
                     reads=(dbuf("h2p"),), writes=(h2b[ysl],))
                return (ha, (h2b[ysl],))

            fm_to_tm_tile(t, get_block, ta, tb[sl])
            ln_rowwise(ta, tb[sl], na, nb[sl], rs.ap(), rs.buf, D,
                       gb=(r_gb.ap(0, 128, 0, 1024), r_gb.ap(0, 128, 1024, 1024),
                           r_gb.buf))
            if tap == "h2":
                store(d_tap[t * 128:(t + 1) * 128, :], na, reads=(nb[sl],),
                      accs=(dbuf("tap"),))
            tm_to_fm_store(t, na, nb[sl], rf, fb, sl, s_h2, dbuf("h2"),
                           bf_pair=(s_h2h, s_h2l), rbf=rbf, bfb=bfb)
        rh2.free(); rt.free(); rn.free(); rs.free(); rf.free(); rbf.free()
        r_gb.free()

    def stage_gating(r_dg):
        r_wg = sba.alloc("gwg", 64)
        r_lg = sba.alloc("glg", 2048)
        r_hs = sba.alloc("ghs", 2 * 512)
        r_tm = sba.alloc("gtm", 24)
        wgb, lgb = r_wg.mkbuf("gwg"), r_lg.mkbuf("glg")
        hsb = r_hs.mkbufs("ghs", 2)
        tmb = r_tm.mkbuf("gtm")
        load(r_wg.ap(0, 128, 0, 64).rearrange("pp (c n) -> pp c n", n=8),
             w_g.rearrange("(c pp) n -> pp c n", pp=128), reads=(W_IN,),
             writes=(wgb,))
        for stripe in range(4):
            q0 = stripe * 512
            for kc in range(NC):
                sl = kc % 2
                ha = r_hs.ap(0, 128, sl * 512, 512)
                load(ha, s_h2[kc * 128:(kc + 1) * 128, q0:q0 + 512],
                     reads=(dbuf("h2"),), writes=(hsb[sl],))
                mm(ps[0][0:8, 0:512], pb[0], r_wg.ap(0, 128, kc * 8, 8), ha,
                   reads=(wgb, hsb[sl]), start=(kc == 0), stop=(kc == NC - 1))
            copy("v", r_lg.ap(0, 8, q0, 512), ps[0][0:8, 0:512],
                 reads=(pb[0],), accs=(lgb,))
        st = r_stats
        for t in range(NT):
            transpose(ps[1][:, 0:8], pb[1], r_lg.ap(0, 8, t * 128, 128),
                      reads=(lgb,), pdim=8)
            la = r_tm.ap(0, 128, 0, 8)
            copy("v", la, ps[1][:, 0:8], reads=(pb[1],), writes=(tmb,))
            mx = st.ap(0, 128, 8, 1)
            z = st.ap(0, 128, 9, 1)
            gt = st.ap(0, 128, 10, 1)
            nmx = st.ap(0, 128, 11, 1)
            p.c("v", lambda e, o=mx, i=la: e.reduce_max(o, i, axis=AX),
                reads=(tmb,), accs=(st.buf,))
            p.c("v", lambda e, o=nmx, i=mx: e.tensor_scalar(
                out=o, in0=i, scalar1=-1.0, scalar2=None, op0=ALU.mult),
                accs=(st.buf,))
            p.c("s", lambda e, o=r_tm.ap(0, 128, 8, 8), i=la, b=nmx, a=z:
                e.activation(o, i, AF.Exp, bias=b, accum_out=a),
                reads=(tmb,), accs=(tmb, st.buf))
            p.c("v", lambda e, o=gt, i=z: e.reciprocal(o, i), accs=(st.buf,))
            p.c("v", lambda e, o=r_tm.ap(0, 128, 16, 8), i=la, a=mx, b=gt:
                e.tensor_scalar(out=o, in0=i, scalar1=a, scalar2=b,
                                op0=ALU.is_equal, op1=ALU.mult),
                reads=(tmb,), accs=(tmb,))
            transpose(ps[2][0:8, 0:128], pb[2], r_tm.ap(0, 128, 16, 8),
                      reads=(tmb,))
            copy("v", r_dg.ap(0, 8, t * 128, 128), ps[2][0:8, 0:128],
                 reads=(pb[2],), accs=(r_dg.buf,))
        r_wg.free(); r_lg.free(); r_hs.free(); r_tm.free()

    def stage_h3(r_h3):
        r_gb = replicate_ln_params(4, 5)
        rh = sba.alloc("s11h", 1024)
        rm3 = sba.alloc("s11m", 1024)
        rt = sba.alloc("s11t", 2048)
        rn = sba.alloc("s11n", 2048)
        rs = sba.alloc("s11s", 1024)
        hbk = rh.mkbufs("s11h", 4)
        mbk = rm3.mkbufs("s11m", 4)
        tb = rt.mkbufs("s11t", 2)
        nb = rn.mkbufs("s11n", 2)
        for t in range(NT):
            sl = t % 2
            ta = rt.ap(0, 128, sl * 1024, 1024)
            na = rn.ap(0, 128, sl * 1024, 1024)

            def get_block(kc, t=t):
                ysl = kc % 4
                ha = rh.ap(0, 128, ysl * 128, 128)
                ma = rm3.ap(0, 128, ysl * 128, 128)
                load(ha, s_h2[kc * 128:(kc + 1) * 128, t * 128:(t + 1) * 128],
                     reads=(dbuf("h2"),), writes=(hbk[ysl],))
                load(ma, s_moe[kc * 128:(kc + 1) * 128, t * 128:(t + 1) * 128],
                     reads=(dbuf("moe"),), writes=(mbk[ysl],), eng="g")
                p.c("v", lambda e, o=ha, a=ma:
                    e.tensor_tensor(out=o, in0=o, in1=a, op=ALU.add),
                    reads=(mbk[ysl],), accs=(hbk[ysl],))
                return ha, (hbk[ysl],)

            fm_to_tm_tile(t, get_block, ta, tb[sl])
            ln_rowwise(ta, tb[sl], na, nb[sl], rs.ap(), rs.buf, D,
                       gb=(r_gb.ap(0, 128, 0, 1024), r_gb.ap(0, 128, 1024, 1024),
                           r_gb.buf))
            if tap == "h3":
                store(d_tap[t * 128:(t + 1) * 128, :], na, reads=(nb[sl],),
                      accs=(dbuf("tap"),))
            for g in range(2):
                pi = 6 + g
                for j in range(4):
                    kc = g * 4 + j
                    transpose(ps[pi][:, j * 128:(j + 1) * 128], pb[pi],
                              na[:, kc * 128:(kc + 1) * 128], reads=(nb[sl],),
                              start=(j == 0))
                for j in range(4):
                    kc = g * 4 + j
                    copy("v", r_h3.ap(0, 128, kc * 2048 + t * 128, 128),
                         ps[pi][:, j * 128:(j + 1) * 128], reads=(pb[pi],),
                         accs=(r_h3.buf,))
        rh.free(); rm3.free(); rt.free(); rn.free(); rs.free(); r_gb.free()

    def stage_out(r_h3):
        r_wp = sba.alloc("owp", 2 * 1024)
        r_fm = sba.alloc("ofm", 2 * 512)
        r_x1 = sba.alloc("ox1", 4 * 128)
        r_ob = sba.alloc("oob", 4 * 128)
        wpb = r_wp.mkbufs("ow", 2)
        fmb = r_fm.mkbufs("of", 2)
        x1b = r_x1.mkbufs("ox", 4)
        obb = r_ob.mkbufs("oo", 4)
        for n in range(NC):
            wsl = n % 2
            load_panel(r_wp, wsl * 1024, w_out, D, n, buf=wpb[wsl])
            for stripe in range(4):
                pi = n % 2
                for kc in range(NC):
                    mm(ps[pi][:, 0:512], pb[pi],
                       r_wp.ap(0, 128, wsl * 1024 + kc * 128, 128),
                       r_h3.ap(0, 128, kc * 2048 + stripe * 512, 512),
                       reads=(wpb[wsl], r_h3.buf), start=(kc == 0),
                       stop=(kc == NC - 1))
                fsl = stripe % 2
                fa = r_fm.ap(0, 128, fsl * 512, 512)
                copy("v", fa, ps[pi][:, 0:512], reads=(pb[pi],), writes=(fmb[fsl],))
                for j in range(4):
                    t0 = stripe * 512 + j * 128
                    pj = 2 + (j % 2)
                    transpose(ps[pj][:, 0:128], pb[pj], fa[:, j * 128:(j + 1) * 128],
                              reads=(fmb[fsl],))
                    xa = r_x1.ap(0, 128, j * 128, 128)
                    load(xa, s_x1[t0:t0 + 128, n * 128:(n + 1) * 128],
                         reads=(dbuf("x1"),), writes=(x1b[j],))
                    oa = r_ob.ap(0, 128, j * 128, 128)
                    p.c("v", lambda e, o=oa, a=xa, b=ps[pj][:, 0:128]:
                        e.tensor_tensor(out=o, in0=a, in1=b, op=ALU.add),
                        reads=(x1b[j], pb[pj]), writes=(obb[j],))
                    store(dout[t0:t0 + 128, n * 128:(n + 1) * 128], oa,
                          reads=(obb[j],), accs=(dbuf("outd"),))
        r_wp.free(); r_fm.free(); r_x1.free(); r_ob.free()

    stage_ln_x()
    if upto in ("h2", "full"):
        split_emit, split_finish = make_split_shards()
    else:
        split_emit, split_finish = (lambda n: None), (lambda: None)
    if upto != "xn":
        r_lat = stage_lat()
        stage_mla(r_lat, split_emit)
        r_lat.free()
        stage_x1_y()
    if upto in ("h1", "h2", "full"):
        stage_R()
        stage_xl(split_emit)
        stage_h1()
    if upto in ("h2", "full"):
        r_acc.free()
        r_fo2 = sba.alloc("ffout", 2 * 512)
        fo2b = r_fo2.mkbufs("ffout", 2)

        def ff_hook(m, stripe, psum_ap, psum_buf, r_b, bb, r_h, hb):
            sl = m % 2
            dst = r_fo2.ap(0, 128, sl * 512, 512)
            p.c("v", lambda e, o=dst, i=psum_ap, b2=r_b.ap(0, 128, 32 + m, 1),
                h1t=r_h.ap(0, 128, m * 512, 512):
                e.scalar_tensor_tensor(out=o, in0=i, scalar=b2, in1=h1t,
                                       op0=ALU.add, op1=ALU.add),
                reads=(psum_buf, bb, hb), writes=(fo2b[sl],))
            store(s_h2p[m * 128:(m + 1) * 128, stripe * 512:stripe * 512 + 512],
                  dst, reads=(fo2b[sl],), accs=(dbuf("h2p"),))

        split_finish()
        ffn_expert(s_h1, s_h1h, s_h1l, dbuf("h1"), s_f1h, s_f1l, s_f2h, s_f2l,
                   dbuf("ws_f1"), lambda n: n, lambda m: m,
                   d_bff1, d_bff2, ff_hook, "ff", need_h32=True)
        r_fo2.free()
        stage_h2()
    if upto == "full":
        r_dg = sba.alloc("dg", 2048)
        stage_gating(r_dg)
        r_rep = sba.alloc("rep", 512)
        r_tmpm = sba.alloc("tmpm", 2 * 512)
        r_macc = sba.alloc("macc", 2 * 512)
        r_sel = sba.alloc("sel", 8 * 128)
        repb = r_rep.mkbuf("rep")
        tmb2 = r_tmpm.mkbufs("tmpm", 2)
        maccb = r_macc.mkbufs("macc", 2)
        selb = r_sel.mkbuf("sel")
        load(r_sel.ap(0, 8, 0, 8 * 128), d_esel, reads=(W_IN,), writes=(selb,))

        for e_idx in range(NE):
            def moe_hook(m, stripe, psum_ap, psum_buf, r_b, bb, r_h, hb,
                         e_idx=e_idx):
                if m == 0:
                    mm(ps[7][:, 0:512], pb[7], r_sel.ap(0, 8, e_idx * 128, 128),
                       r_dg.ap(0, 8, stripe * 512, 512),
                       reads=(selb, r_dg.buf), start=True, stop=True)
                    copy("v", r_rep.ap(0, 128, 0, 512), ps[7][:, 0:512],
                         reads=(pb[7],), writes=(repb,))
                sl = m % 2
                dst = r_tmpm.ap(0, 128, sl * 512, 512)
                dram = s_moe[m * 128:(m + 1) * 128, stripe * 512:stripe * 512 + 512]
                if e_idx == 0:
                    p.c("v", lambda e, o=dst, i=psum_ap,
                        b2=r_b.ap(0, 128, 32 + m, 1), rp=r_rep.ap(0, 128, 0, 512):
                        e.scalar_tensor_tensor(out=o, in0=i, scalar=b2, in1=rp,
                                               op0=ALU.add, op1=ALU.mult),
                        reads=(psum_buf, bb, repb), writes=(tmb2[sl],))
                else:
                    prev = r_macc.ap(0, 128, sl * 512, 512)
                    load(prev, dram, reads=(dbuf("moe"),), writes=(maccb[sl],))
                    p.c("v", lambda e, o=dst, i=psum_ap,
                        b2=r_b.ap(0, 128, 32 + m, 1), rp=r_rep.ap(0, 128, 0, 512):
                        e.scalar_tensor_tensor(out=o, in0=i, scalar=b2, in1=rp,
                                               op0=ALU.add, op1=ALU.mult),
                        reads=(psum_buf, bb, repb), writes=(tmb2[sl],))
                    p.c("v", lambda e, o=dst, i=prev:
                        e.tensor_tensor(out=o, in0=o, in1=i, op=ALU.add),
                        reads=(maccb[sl],), accs=(tmb2[sl],))
                store(dram, dst, reads=(tmb2[sl],), accs=(dbuf("moe"),))

            ffn_expert(s_h2, s_h2h, s_h2l, dbuf("h2"), s_e1h, s_e1l,
                       s_e2h, s_e2l, dbuf(f"ws_e1_{e_idx}"),
                       lambda n, e_idx=e_idx: (e_idx, n),
                       lambda m, e_idx=e_idx: (e_idx, m),
                       d_be1[e_idx], d_be2[e_idx], moe_hook, f"moe{e_idx}",
                       need_h32=False)
        r_dg.free(); r_rep.free(); r_tmpm.free(); r_macc.free(); r_sel.free()

        r_h3 = sba.alloc("h3", 16384)
        stage_h3(r_h3)
        stage_out(r_h3)
        r_h3.free()

    if upto == "full":
        p.c("g", None, reads=(dbuf("outd"),))
    if tap is not None:
        p.c("g", None, reads=(dbuf("tap"),))

    sem_handles = {}
    import contextlib
    stack = contextlib.ExitStack()
    keys = set()
    for o in p.ops:
        keys.add(o.dcls if o.kind == "d" else (o.eng, "c"))
    for key in sorted(keys, key=str):
        sem_handles[key] = stack.enter_context(
            nc.semaphore(f"sem_{key[0]}_{key[1]}"))
    with nc.Block() as block:
        p.finalize_and_emit(nc, block, sem_handles)
    stack.close()
    return nc


def kernel(**inputs):
    gi = inputs

    def P(name):
        return _panels(gi[name])

    decay, peT, ident, tri, esel = _host_consts()
    shared = {
        "w_q": P("L0_attn_Wq"), "w_k": P("L0_attn_Wk"), "w_v": P("L0_attn_Wv"),
        "w_o": P("L0_attn_Wo"), "w_dn": P("L0_attn_W_down_kv"),
        "w_uk": P("L0_attn_W_up_k"), "w_uv": P("L0_attn_W_up_v"),
        "w_qkv": P("L0_ffn_Wqkv"), "w_fo": P("L0_ffn_Wo"), "w_r": P("L0_ffn_Wr"),
        "u_bias": np.ascontiguousarray(np.asarray(
            gi["L0_ffn_u_bias"], np.float32).reshape(8, 128).T),
        "v_bias": np.ascontiguousarray(np.asarray(
            gi["L0_ffn_v_bias"], np.float32).reshape(8, 128).T),
        "ln_gb": np.ascontiguousarray(np.stack([
            gi["L0_ffn_ln1_g"], gi["L0_ffn_ln1_b"], gi["L0_ffn_ln2_g"],
            gi["L0_ffn_ln2_b"], gi["L0_ffn_ln3_g"], gi["L0_ffn_ln3_b"]]), np.float32),
        "w_ff1": P("L0_ffn_Wff1"),
        "b_ff1": np.ascontiguousarray(np.asarray(
            gi["L0_ffn_bff1"], np.float32).reshape(32, 128).T),
        "w_ff2": P("L0_ffn_Wff2"),
        "b_ff2": np.ascontiguousarray(np.asarray(
            gi["L0_ffn_bff2"], np.float32).reshape(8, 128).T),
        "w_g": np.ascontiguousarray(gi["L0_ffn_Wg"], np.float32),
        "w_e1": np.stack([_panels(gi["L0_ffn_We1"][e]) for e in range(NE)]),
        "b_e1": np.ascontiguousarray(np.asarray(
            gi["L0_ffn_be1"], np.float32).reshape(NE, 32, 128).transpose(0, 2, 1)),
        "w_e2": np.stack([_panels(gi["L0_ffn_We2"][e]) for e in range(NE)]),
        "b_e2": np.ascontiguousarray(np.asarray(
            gi["L0_ffn_be2"], np.float32).reshape(NE, 8, 128).transpose(0, 2, 1)),
        "w_out": P("L0_ffn_Wout"),
        "c_decay": decay, "c_peT": peT, "c_ident": ident, "c_tri": tri,
        "c_esel": esel,
    }
    x_full = np.ascontiguousarray(np.asarray(gi["L0_attn_x"], np.float32))
    nc = build_program(tap=None, upto="full")
    in_maps = [dict(shared, x=x_full[i]) for i in range(B)]
    res = run_bass_kernel_spmd(nc, in_maps, core_ids=list(range(B)))
    return np.stack([r["out"] for r in res.results], axis=0)
